# version 02
# baseline (speedup 1.0000x reference)
import numpy as np
import ml_dtypes
from contextlib import ExitStack

import concourse.bass as bass
import concourse.mybir as mybir
from concourse.bass_utils import run_bass_kernel_spmd

F32 = mybir.dt.float32
BF16 = mybir.dt.bfloat16
I32 = mybir.dt.int32
ALU = mybir.AluOpType
ACTF = mybir.ActivationFunctionType
AX = mybir.AxisListType

B, T, C = 2, 4096, 4096
NH, HD = 32, 128
HPC = 8
HPP = 4
TQ = 512
NT = T // TQ
NST = T // 128
NCC = C // 128
TOK = 1024
HID = 11008
NHC = 86
NHC2 = 43
SCL = 1.0 / float(np.sqrt(128.0))
BND_PAD = 1.02
EPS = 1e-5

_UPTO = 6
_DBG_OUT = False


class Sem:
    __slots__ = ("h", "n")

    def __init__(self, stack, nc, name, reg):
        self.h = stack.enter_context(nc.semaphore(name))
        self.n = 0
        reg.append(self)

    def inc(self, inst, k=1):
        inst.then_inc(self.h, k)
        self.n += k

    def dma(self, inst):
        inst.then_inc(self.h, 16)
        self.n += 16


class Slot:

    def __init__(self, k, seed=None):
        self.g = [list(seed or []) for _ in range(k)]
        self.k = k

    def acquire(self, eng, i):
        for (s, v) in self.g[i % self.k]:
            eng.wait_ge(s.h, v)
        self.g[i % self.k] = []

    def release(self, i, sem):
        self.g[i % self.k].append((sem, sem.n))


def build(sg, su, sd, upto=6, dbg=False):
    nc = bass.Bass("TRN2", target_bir_lowering=False)

    d_x = nc.dram_tensor("x_tok", [T, C], F32, kind="ExternalInput")
    d_xT = nc.dram_tensor("xT_own", [C, TOK], F32, kind="ExternalInput")
    d_wk = [nc.dram_tensor(f"wkT_p{p}", [C, HPP * HD], F32, kind="ExternalInput") for p in range(2)]
    d_wq = [nc.dram_tensor(f"wqT_p{p}", [C, HPP * HD], F32, kind="ExternalInput") for p in range(2)]
    d_wv = nc.dram_tensor("wvT_g", [C, HPC * HD], F32, kind="ExternalInput")
    d_wo = nc.dram_tensor("woT_mine", [HPC * HD, C], F32, kind="ExternalInput")
    d_wg = nc.dram_tensor("wg_t", [NHC2, NCC, 128, 256], I32, kind="ExternalInput")
    d_wu = nc.dram_tensor("wu_t", [NHC2, NCC, 128, 256], I32, kind="ExternalInput")
    d_wd = nc.dram_tensor("wd_t", [NCC, NHC, 128, 128], I32, kind="ExternalInput")
    d_sin = nc.dram_tensor("sin_t", [HD, T], F32, kind="ExternalInput")
    d_cos = nc.dram_tensor("cos_t", [HD, T], F32, kind="ExternalInput")
    d_rt = nc.dram_tensor("rope_rt", [HD, HD], BF16, kind="ExternalInput")
    d_msk = nc.dram_tensor("msk", [128, 1024], F32, kind="ExternalInput")
    d_idx = nc.dram_tensor("quarter_idx", [1, 1], I32, kind="ExternalInput")

    d_y = nc.dram_tensor("y_out", [C, TOK], F32, kind="ExternalOutput")
    sk = dict(kind="ExternalOutput") if dbg else {}
    d_xn = nc.dram_tensor("xn_sc", [NST, 128, NCC * 128], BF16, **sk)
    d_qs = nc.dram_tensor("q_sc", [HPC, HD, T], BF16, **sk)
    d_ks = nc.dram_tensor("k_sc", [HPC, HD, T], BF16, **sk)
    d_mid = nc.dram_tensor("mid_sc", [NHC, 128, TOK], BF16, **sk)
    d_cci = nc.dram_tensor("cc_in", [4, C, TOK], BF16)
    d_cco = nc.dram_tensor("cc_out", [C, TOK], BF16)
    if dbg:
        d_vdb = nc.dram_tensor("v_dbg", [128, NST * HPC * HD], BF16, kind="ExternalOutput")
        d_ccid = nc.dram_tensor("cci_dbg", [4, C, TOK], BF16, kind="ExternalOutput")
        d_ccod = nc.dram_tensor("cco_dbg", [C, TOK], BF16, kind="ExternalOutput")
        d_bdb = nc.dram_tensor("b_dbg", [128, 64], F32, kind="ExternalOutput")

    gs = ExitStack()
    with gs:
        ps = [gs.enter_context(nc.psum_tensor(f"ps{i}", [128, 512], F32)) for i in range(8)]

        SEMS = []
        s_pe = Sem(gs, nc, "s_pe", SEMS)
        s_dve = Sem(gs, nc, "s_dve", SEMS)
        s_act = Sem(gs, nc, "s_act", SEMS)
        s_cc = Sem(gs, nc, "s_cc", [])
        m_init = Sem(gs, nc, "m_init", SEMS)
        m_x = [Sem(gs, nc, f"m_x{i}", SEMS) for i in range(2)]
        m_xb = [Sem(gs, nc, f"m_xb{i}", SEMS) for i in range(2)]
        m_st = [Sem(gs, nc, f"m_st{i}", SEMS) for i in range(2)]
        m_wv = Sem(gs, nc, "m_wv", SEMS)
        m_xnt = Sem(gs, nc, "m_xnt", SEMS)
        m_rt = Sem(gs, nc, "m_rt", SEMS)
        m_sc = [Sem(gs, nc, f"m_sc{i}", SEMS) for i in range(2)]
        m_wkq = Sem(gs, nc, "m_wkq", SEMS)
        m_kst = [Sem(gs, nc, f"m_kst{i}", SEMS) for i in range(2)]
        m_wo = Sem(gs, nc, "m_wo", SEMS)
        m_msk = Sem(gs, nc, "m_msk", SEMS)
        m_q = [Sem(gs, nc, f"m_q{i}", SEMS) for i in range(2)]
        m_k = [Sem(gs, nc, f"m_k{i}", SEMS) for i in range(2)]
        m_cci = [Sem(gs, nc, f"m_cci{i}", SEMS) for i in range(2)]
        m_xno = Sem(gs, nc, "m_xno", SEMS)
        m_wgu = [Sem(gs, nc, f"m_wgu{i}", SEMS) for i in range(2)]
        m_mid = [Sem(gs, nc, f"m_mid{i}", SEMS) for i in range(4)]
        m_wd = [Sem(gs, nc, f"m_wd{i}", SEMS) for i in range(2)]
        m_m6 = Sem(gs, nc, "m_m6", SEMS)
        m_ep = [Sem(gs, nc, f"m_ep{i}", SEMS) for i in range(2)]
        m_y = [Sem(gs, nc, f"m_y{i}", SEMS) for i in range(2)]
        m_dbg = Sem(gs, nc, "m_dbg", SEMS) if dbg else None

        def fence(blk):
            vals = [(s.h, s.n) for s in SEMS if s.n > 0]

            @blk.sync
            def _(sq):
                for h, v in vals:
                    sq.wait_ge(h, v)

            @blk.scalar
            def _(sc):
                for h, v in vals:
                    sc.wait_ge(h, v)

            @blk.gpsimd
            def _(g):
                for h, v in vals:
                    g.wait_ge(h, v)

        def seeded(k):
            return Slot(k, seed=[(s_dve, s_dve.n), (s_act, s_act.n), (s_pe, s_pe.n)])

        ones_col = gs.enter_context(nc.sbuf_tensor("ones_col", [128, 1], BF16))
        ones_rb = gs.enter_context(nc.sbuf_tensor("ones_rb", [1, 128], BF16))
        ones_rf = gs.enter_context(nc.sbuf_tensor("ones_rf", [1, 128], F32))
        eps_sb = gs.enter_context(nc.sbuf_tensor("eps_sb", [128, 1], F32))
        bias_sb = gs.enter_context(nc.sbuf_tensor("bias_sb", [128, 64], F32))
        bq_sb = gs.enter_context(nc.sbuf_tensor("bq_sb", [1, 64], F32))
        bk_sb = gs.enter_context(nc.sbuf_tensor("bk_sb", [1, 8], F32))
        idx_sb = gs.enter_context(nc.sbuf_tensor("idx_sb", [1, 1], I32))
        t0a = gs.enter_context(nc.sbuf_tensor("t0a", [1, 1], F32))
        t0b = gs.enter_context(nc.sbuf_tensor("t0b", [1, 1], F32))
        sumx = gs.enter_context(nc.sbuf_tensor("sumx", [128, 1], F32))
        sumsq = gs.enter_context(nc.sbuf_tensor("sumsq", [128, 1], F32))
        negm = gs.enter_context(nc.sbuf_tensor("negm", [128, 1], F32))
        varr = gs.enter_context(nc.sbuf_tensor("varr", [128, 1], F32))
        m2t = gs.enter_context(nc.sbuf_tensor("m2t", [128, 1], F32))
        sqv = gs.enter_context(nc.sbuf_tensor("sqv", [128, 1], F32))
        rinv = gs.enter_context(nc.sbuf_tensor("rinv", [128, 1], F32))

        with nc.Block() as blk:
            @blk.vector
            def _(v):
                v.memset(ones_col[:, :], 1.0)
                v.memset(ones_rb[:, :], 1.0)
                v.memset(ones_rf[:, :], 1.0)
                v.memset(bq_sb[:, :], 0.0)
                v.memset(bk_sb[:, :], 0.0)
                s_dve.inc(v.memset(eps_sb[:, :], EPS))

            @blk.gpsimd
            def _(g):
                m_init.dma(g.dma_start(out=idx_sb[:, :], in_=d_idx[:, :]))

        with ExitStack() as st1:
            x_sb = [st1.enter_context(nc.sbuf_tensor(f"x_sb{i}", [128, C], F32)) for i in range(2)]
            junk = st1.enter_context(nc.sbuf_tensor("junk", [128, C], BF16))
            xn_sb = [st1.enter_context(nc.sbuf_tensor(f"xn_sb{i}", [128, C], BF16)) for i in range(2)]
            xnf_sb = [st1.enter_context(nc.sbuf_tensor(f"xnf_sb{i}", [128, C], BF16)) for i in range(2)]
            sl_x, sl_xn, sl_xnf = Slot(2), Slot(2), Slot(2)
            dve_stats = {}

            with nc.Block() as blk:
                for t8 in range(NST):
                    @blk.sync
                    def _(sq, t8=t8):
                        sl_x.acquire(sq, t8)
                        m_x[t8 % 2].dma(sq.dma_start(out=x_sb[t8 % 2][:, :],
                                                     in_=d_x[t8 * 128:(t8 + 1) * 128, :]))
                    ldx_n = m_x[t8 % 2].n

                    @blk.scalar
                    def _(sc, t8=t8, ldx_n=ldx_n):
                        sc.wait_ge(m_x[t8 % 2].h, ldx_n)
                        if t8 == 0:
                            sc.wait_ge(s_dve.h, 1)
                        else:
                            sc.wait_ge(s_dve.h, dve_stats[t8 - 1])
                        s_act.inc(sc.activation(junk[:, :], x_sb[t8 % 2][:, :], ACTF.Square,
                                                accum_out=sumsq[:, :]))
                        s_act.inc(sc.activation(junk[:, :], x_sb[t8 % 2][:, :], ACTF.Copy,
                                                accum_out=sumx[:, :]))
                    acc_n = s_act.n

                    @blk.vector
                    def _(v, acc_n=acc_n):
                        v.wait_ge(s_act.h, acc_n)
                        v.tensor_scalar(negm[:, :], sumx[:, :], -1.0 / C, None, ALU.mult)
                        s_dve.inc(v.tensor_scalar(varr[:, :], sumsq[:, :], 1.0 / C, None, ALU.mult))
                        v.wait_ge(s_dve.h, s_dve.n)
                        s_dve.inc(v.tensor_mul(m2t[:, :], negm[:, :], negm[:, :]))
                        v.wait_ge(s_dve.h, s_dve.n)
                        s_dve.inc(v.tensor_sub(varr[:, :], varr[:, :], m2t[:, :]))
                    var_n = s_dve.n

                    @blk.scalar
                    def _(sc, var_n=var_n):
                        sc.wait_ge(s_dve.h, var_n)
                        s_act.inc(sc.activation(sqv[:, :], varr[:, :], ACTF.Sqrt, bias=eps_sb[:, :]))
                    sq_n = s_act.n

                    @blk.vector
                    def _(v, t8=t8, sq_n=sq_n):
                        v.wait_ge(s_act.h, sq_n)
                        s_dve.inc(v.reciprocal(rinv[:, :], sqv[:, :]))
                        v.wait_ge(s_dve.h, s_dve.n)
                        sl_xn.acquire(v, t8)
                        with nc.allow_low_precision(reason="bf16 matmul operand stream"):
                            s_dve.inc(v.tensor_scalar(xn_sb[t8 % 2][:, :], x_sb[t8 % 2][:, :],
                                                      negm[:, :], rinv[:, :], ALU.add, ALU.mult))
                    xn_n = s_dve.n
                    dve_stats[t8] = xn_n
                    sl_x.release(t8, s_act)
                    sl_x.release(t8, s_dve)

                    @blk.scalar
                    def _(sc, t8=t8, xn_n=xn_n):
                        sc.wait_ge(s_dve.h, xn_n)
                        sl_xnf.acquire(sc, t8)
                        for cc in range(NCC):
                            m_xb[t8 % 2].dma(sc.dma_start(out=xnf_sb[t8 % 2][:, cc * 128:(cc + 1) * 128],
                                                          in_=xn_sb[t8 % 2][:, cc * 128:(cc + 1) * 128],
                                                          transpose=True))
                    xba_n = m_xb[t8 % 2].n

                    @blk.sync
                    def _(sq, t8=t8, xba_n=xba_n):
                        sq.wait_ge(m_xb[t8 % 2].h, xba_n)
                        m_st[t8 % 2].dma(sq.dma_start(
                            out=d_xn.ap()[t8],
                            in_=xnf_sb[t8 % 2][:, :]))
                    sl_xn.release(t8, m_xb[t8 % 2])
                    sl_xnf.release(t8, m_st[t8 % 2])

        if upto < 2:
            _finish(nc, SEMS)
            return nc

        with ExitStack() as stA:
            V_sb = stA.enter_context(nc.sbuf_tensor("V_sb", [128, NST * HPC * HD], BF16))
            xnb = stA.enter_context(nc.sbuf_tensor("xnb", [128, NCC * TQ], BF16))
            sl_xnb = Slot(1)

            def load_xn_tile(sq, tt):
                for k in range(4):
                    t8 = 4 * tt + k
                    m_xnt.dma(sq.dma_start(
                        out=xnb[:, k * (NCC * 128):(k + 1) * (NCC * 128)],
                        in_=d_xn.ap()[t8]))

            with ExitStack() as stV:
                wv_sb = stV.enter_context(nc.sbuf_tensor("wv_sb", [128, NCC * HPC * HD], BF16))
                sl_psV = seeded(2)

                with nc.Block() as blk:
                    fence(blk)

                    @blk.gpsimd
                    def _(g):
                        m_wv.dma(g.dma_start(out=wv_sb[:, :].rearrange("p (a d) -> p a d", a=NCC),
                                             in_=d_wv.ap().rearrange("(a p) d -> p a d", p=128)))
                    wv_n = m_wv.n

                    for tt in range(NT):
                        @blk.sync
                        def _(sq, tt=tt):
                            sl_xnb.acquire(sq, 0)
                            load_xn_tile(sq, tt)
                        ld_n = m_xnt.n

                        for ts4 in range(4):
                            i = tt * 4 + ts4

                            @blk.tensor
                            def _(te, tt=tt, ts4=ts4, i=i, ld_n=ld_n):
                                te.wait_ge(m_xnt.h, ld_n)
                                if i == 0:
                                    te.wait_ge(m_wv.h, wv_n)
                                sl_psV.acquire(te, i)
                                pA, pB = ps[2 * (i % 2)], ps[2 * (i % 2) + 1]
                                mm = None
                                for cc in range(NCC):
                                    for dh, pX in ((0, pA), (1, pB)):
                                        mm = te.matmul(pX[:, :],
                                                       xnb[:, ts4 * 4096 + cc * 128: ts4 * 4096 + (cc + 1) * 128],
                                                       wv_sb[:, cc * 1024 + dh * 512: cc * 1024 + (dh + 1) * 512],
                                                       start=(cc == 0), stop=(cc == NCC - 1))
                                s_pe.inc(mm)
                            mm_n = s_pe.n

                            @blk.vector
                            def _(v, tt=tt, ts4=ts4, i=i, mm_n=mm_n):
                                v.wait_ge(s_pe.h, mm_n)
                                pA, pB = ps[2 * (i % 2)], ps[2 * (i % 2) + 1]
                                t8 = tt * 4 + ts4
                                with nc.allow_low_precision(reason="V operand bf16"):
                                    v.tensor_copy(V_sb[:, t8 * 1024: t8 * 1024 + 512], pA[:, :])
                                    s_dve.inc(v.tensor_copy(V_sb[:, t8 * 1024 + 512: t8 * 1024 + 1024],
                                                            pB[:, :]))
                            sl_psV.release(i, s_dve)
                        sl_xnb.release(0, s_pe)
                    if dbg:
                        @blk.sync
                        def _(sq):
                            sq.wait_ge(s_dve.h, s_dve.n)
                            m_dbg.dma(sq.dma_start(out=d_vdb[:, :], in_=V_sb[:, :]))

            with ExitStack() as stK:
                wk_sb = stK.enter_context(nc.sbuf_tensor("wk_sb", [128, NCC * HPP * HD], BF16))
                wq_sb = stK.enter_context(nc.sbuf_tensor("wq_sb", [128, NCC * HPP * HD], BF16))
                rt_sb = stK.enter_context(nc.sbuf_tensor("rt_sb", [HD, HD], BF16))
                sin_sb = [stK.enter_context(nc.sbuf_tensor(f"sin_sb{i}", [HD, TQ], F32)) for i in range(2)]
                cos_sb = [stK.enter_context(nc.sbuf_tensor(f"cos_sb{i}", [HD, TQ], F32)) for i in range(2)]
                ks_sb = [stK.enter_context(nc.sbuf_tensor(f"ks_sb{i}", [128, TQ], BF16)) for i in range(2)]
                kc_sb = [stK.enter_context(nc.sbuf_tensor(f"kc_sb{i}", [128, TQ], BF16)) for i in range(2)]
                kf_sb = [stK.enter_context(nc.sbuf_tensor(f"kf_sb{i}", [128, TQ], BF16)) for i in range(2)]
                k2_sb = stK.enter_context(nc.sbuf_tensor("k2_sb", [128, TQ], BF16))
                sl_w = Slot(1)
                sl_k2 = Slot(1)
                sl_sct = Slot(2)
                sl_psK = seeded(2)
                sl_psR = seeded(2)
                sl_psN = seeded(1)
                sl_ks = Slot(2)
                sl_kf = Slot(2)

                with nc.Block() as blk:
                    fence(blk)

                    @blk.sync
                    def _(sq):
                        m_rt.dma(sq.dma_start(out=rt_sb[:, :], in_=d_rt[:, :]))
                    rt_n = m_rt.n

                    ri = 0
                    for p in range(2):
                        @blk.gpsimd
                        def _(g, p=p):
                            sl_w.acquire(g, 0)
                            m_wkq.dma(g.dma_start(out=wk_sb[:, :].rearrange("p (a d) -> p a d", a=NCC),
                                                  in_=d_wk[p].ap().rearrange("(a p) d -> p a d", p=128)))
                            m_wkq.dma(g.dma_start(out=wq_sb[:, :].rearrange("p (a d) -> p a d", a=NCC),
                                                  in_=d_wq[p].ap().rearrange("(a p) d -> p a d", p=128)))
                        w_n = m_wkq.n

                        for tt in range(NT):
                            @blk.sync
                            def _(sq, tt=tt):
                                sl_xnb.acquire(sq, 0)
                                load_xn_tile(sq, tt)
                                sl_sct.acquire(sq, tt)
                                m_sc[tt % 2].dma(sq.dma_start(out=sin_sb[tt % 2][:, :],
                                                              in_=d_sin[:, tt * TQ:(tt + 1) * TQ]))
                                m_sc[tt % 2].dma(sq.dma_start(out=cos_sb[tt % 2][:, :],
                                                              in_=d_cos[:, tt * TQ:(tt + 1) * TQ]))
                            ld_n = m_xnt.n
                            sc_n = m_sc[tt % 2].n

                            for proj, w_t, d_out, b_t in (("k", wk_sb, d_ks, bk_sb),
                                                          ("q", wq_sb, d_qs, bq_sb)):
                                for dt in range(HPP):
                                    h = HPP * p + dt
                                    my = ri
                                    ri += 1

                                    @blk.tensor
                                    def _(te, w_t=w_t, dt=dt, my=my, ld_n=ld_n, w_n=w_n):
                                        te.wait_ge(m_xnt.h, ld_n)
                                        te.wait_ge(m_wkq.h, w_n)
                                        sl_psK.acquire(te, my)
                                        pK = ps[my % 2]
                                        xnr = xnb[:, :].rearrange("p (k a t) -> p k a t", k=4, a=NCC)
                                        mm = None
                                        for cc in range(NCC):
                                            mm = te.matmul(pK[:, :],
                                                           w_t[:, cc * (HPP * HD) + dt * 128:
                                                               cc * (HPP * HD) + (dt + 1) * 128],
                                                           xnr[:, :, cc, :],
                                                           start=(cc == 0), stop=(cc == NCC - 1))
                                        s_pe.inc(mm)
                                    mm_n = s_pe.n

                                    @blk.vector
                                    def _(v, tt=tt, my=my, mm_n=mm_n, sc_n=sc_n):
                                        v.wait_ge(s_pe.h, mm_n)
                                        v.wait_ge(m_sc[tt % 2].h, sc_n)
                                        sl_ks.acquire(v, my)
                                        pK = ps[my % 2]
                                        with nc.allow_low_precision(reason="rope bf16"):
                                            v.tensor_mul(ks_sb[my % 2][:, :], pK[:, :], sin_sb[tt % 2][:, :])
                                            s_dve.inc(v.tensor_mul(kc_sb[my % 2][:, :], pK[:, :],
                                                                   cos_sb[tt % 2][:, :]))
                                    mul_n = s_dve.n
                                    sl_psK.release(my, s_dve)

                                    @blk.tensor
                                    def _(te, my=my, mul_n=mul_n, rt_n=rt_n):
                                        te.wait_ge(s_dve.h, mul_n)
                                        if my == 0:
                                            te.wait_ge(m_rt.h, rt_n)
                                        sl_psR.acquire(te, my)
                                        s_pe.inc(te.matmul(ps[2 + my % 2][:, :], rt_sb[:, :],
                                                           ks_sb[my % 2][:, :], start=True, stop=True))
                                    rmm_n = s_pe.n

                                    @blk.vector
                                    def _(v, my=my, rmm_n=rmm_n):
                                        v.wait_ge(s_pe.h, rmm_n)
                                        sl_kf.acquire(v, my)
                                        with nc.allow_low_precision(reason="rope bf16"):
                                            s_dve.inc(v.tensor_add(kf_sb[my % 2][:, :], ps[2 + my % 2][:, :],
                                                                   kc_sb[my % 2][:, :]))
                                    kf_n = s_dve.n
                                    sl_ks.release(my, s_dve)
                                    sl_psR.release(my, s_dve)

                                    @blk.scalar
                                    def _(sc, my=my, kf_n=kf_n):
                                        sc.wait_ge(s_dve.h, kf_n)
                                        sl_k2.acquire(sc, 0)
                                        s_act.inc(sc.activation(k2_sb[:, :], kf_sb[my % 2][:, :], ACTF.Square))
                                    k2_n = s_act.n

                                    @blk.tensor
                                    def _(te, k2_n=k2_n):
                                        te.wait_ge(s_act.h, k2_n)
                                        sl_psN.acquire(te, 0)
                                        s_pe.inc(te.matmul(ps[4][0:1, :], ones_col[:, :], k2_sb[:, :],
                                                           start=True, stop=True))
                                    nrm_n = s_pe.n
                                    sl_k2.release(0, s_pe)

                                    @blk.vector
                                    def _(v, h=h, tt=tt, proj=proj, b_t=b_t, nrm_n=nrm_n):
                                        v.wait_ge(s_pe.h, nrm_n)
                                        if proj == "q":
                                            s_dve.inc(v.reduce_max(bq_sb[0:1, h * 8 + tt: h * 8 + tt + 1],
                                                                   ps[4][0:1, :], axis=AX.X))
                                            v.wait_ge(s_dve.h, s_dve.n)
                                        else:
                                            s_dve.inc(v.reduce_max(t0a[:, :], ps[4][0:1, :], axis=AX.X))
                                            v.wait_ge(s_dve.h, s_dve.n)
                                            s_dve.inc(v.tensor_max(b_t[0:1, h:h + 1], b_t[0:1, h:h + 1],
                                                                   t0a[:, :]))
                                            v.wait_ge(s_dve.h, s_dve.n)
                                    sl_psN.release(0, s_dve)

                                    @blk.sync
                                    def _(sq, h=h, tt=tt, my=my, d_out=d_out, kf_n=kf_n):
                                        sq.wait_ge(s_dve.h, kf_n)
                                        m_kst[my % 2].dma(sq.dma_start(
                                            out=d_out.ap()[h][:, tt * TQ:(tt + 1) * TQ],
                                            in_=kf_sb[my % 2][:, :]))
                                    sl_kf.release(my, m_kst[my % 2])
                            sl_xnb.release(0, s_pe)
                            sl_sct.release(tt, s_dve)
                        sl_w.release(0, s_pe)

            if upto < 3:
                _finish(nc, SEMS)
                return nc

            with nc.Block() as blk:
                sl_psB5 = seeded(1)
                prev_bm = s_pe.n
                for hq in range(HPC * 8):
                    h = hq // 8
                    @blk.vector
                    def _(v, h=h, hq=hq, prev_bm=prev_bm):
                        if hq == 0:
                            v.wait_ge(s_dve.h, s_dve.n)
                        else:
                            v.wait_ge(s_pe.h, prev_bm)
                        s_dve.inc(v.tensor_mul(t0a[:, :], bq_sb[0:1, hq:hq + 1], bk_sb[0:1, h:h + 1]))
                    m_n = s_dve.n

                    @blk.scalar
                    def _(sc, m_n=m_n):
                        sc.wait_ge(s_dve.h, m_n)
                        s_act.inc(sc.activation(t0b[:, :], t0a[:, :], ACTF.Sqrt))
                    sq_n = s_act.n

                    @blk.vector
                    def _(v, sq_n=sq_n):
                        v.wait_ge(s_act.h, sq_n)
                        s_dve.inc(v.tensor_scalar(t0a[:, :], t0b[:, :], -SCL * BND_PAD, 40.0,
                                                  ALU.mult, ALU.add))
                        v.wait_ge(s_dve.h, s_dve.n)
                    bb_n = s_dve.n

                    @blk.tensor
                    def _(te, bb_n=bb_n):
                        te.wait_ge(s_dve.h, bb_n)
                        sl_psB5.acquire(te, 0)
                        s_pe.inc(te.matmul(ps[5][:, 0:1], ones_rf[:, :], t0a[:, :],
                                           start=True, stop=True))
                    bm_n = s_pe.n
                    prev_bm = bm_n

                    @blk.vector
                    def _(v, hq=hq, bm_n=bm_n):
                        v.wait_ge(s_pe.h, bm_n)
                        s_dve.inc(v.tensor_copy(bias_sb[:, hq:hq + 1], ps[5][:, 0:1]))
                    sl_psB5.release(0, s_dve)
                if dbg:
                    @blk.sync
                    def _(sq):
                        sq.wait_ge(s_dve.h, s_dve.n)
                        m_dbg.dma(sq.dma_start(out=d_bdb[:, :], in_=bias_sb[:, :]))

            with ExitStack() as st3:
                woT_sb = st3.enter_context(nc.sbuf_tensor("woT_sb", [128, HPC * C], BF16))
                msk_sb = st3.enter_context(nc.sbuf_tensor("msk_sb", [128, 1024], F32))
                kbuf = [st3.enter_context(nc.sbuf_tensor(f"kbuf{i}", [128, T], BF16)) for i in range(2)]
                q_sb = [st3.enter_context(nc.sbuf_tensor(f"q_sb{i}", [128, TQ], BF16)) for i in range(2)]
                pt_sb = [st3.enter_context(nc.sbuf_tensor(f"pt_sb{i}", [128, TQ], BF16)) for i in range(3)]
                attn_sb = st3.enter_context(nc.sbuf_tensor("attn_sb", [128, HPC * TQ], BF16))
                rb_sb = st3.enter_context(nc.sbuf_tensor("rb_sb", [1, TQ], BF16))
                rg_sb = st3.enter_context(nc.sbuf_tensor("rg_sb", [1, TQ], F32))
                rbc_sb = st3.enter_context(nc.sbuf_tensor("rbc_sb", [128, TQ], F32))
                yp_sb = [st3.enter_context(nc.sbuf_tensor(f"yp_sb{i}", [128, TQ], BF16)) for i in range(2)]
                sl_q, sl_k = Slot(2), Slot(2)
                PSAB = [0, 1, 7]
                sl_psAB = seeded(3)
                sl_pt = Slot(3)
                sl_psO = seeded(2)
                sl_psr = seeded(2)
                sl_psB = seeded(1)
                sl_attn = Slot(1)
                sl_yp = Slot(2)

                iters = [(qb, h) for qb in range(NT) for h in range(HPC)]
                ldq_n, ldk_n = {}, {}

                def p3_loads(blk, li):
                    qb, h = iters[li]

                    @blk.sync
                    def _(sq, li=li, qb=qb, h=h):
                        sl_q.acquire(sq, li)
                        m_q[li % 2].dma(sq.dma_start(out=q_sb[li % 2][:, :],
                                                     in_=d_qs.ap()[h][:, qb * TQ:(qb + 1) * TQ]))
                    ldq_n[li] = m_q[li % 2].n

                    @blk.gpsimd
                    def _(g, li=li, qb=qb, h=h):
                        sl_k.acquire(g, li)
                        pl = TQ * (qb + 1)
                        m_k[li % 2].dma(g.dma_start(out=kbuf[li % 2][:, 0:pl],
                                                    in_=d_ks.ap()[h][:, 0:pl]))
                    ldk_n[li] = m_k[li % 2].n

                with nc.Block() as blk:
                    fence(blk)

                    @blk.gpsimd
                    def _(g):
                        m_wo.dma(g.dma_start(out=woT_sb[:, :].rearrange("p (a d) -> p a d", a=HPC),
                                             in_=d_wo.ap().rearrange("(a p) d -> p a d", p=128)))
                    wo_n = m_wo.n

                    @blk.sync
                    def _(sq):
                        m_msk.dma(sq.dma_start(out=msk_sb[:, :], in_=d_msk[:, :]))
                    mk_n = m_msk.n

                    p3_loads(blk, 0)

                    kt_i = 0
                    for li, (qb, h) in enumerate(iters):
                        nkt = 4 * (qb + 1)
                        for kt in range(nkt):
                            my_kt = kt_i
                            kt_i += 1

                            @blk.tensor
                            def _(te, li=li, kt=kt, my_kt=my_kt):
                                if kt == 0:
                                    te.wait_ge(m_q[li % 2].h, ldq_n[li])
                                    te.wait_ge(m_k[li % 2].h, ldk_n[li])
                                sl_psAB.acquire(te, my_kt)
                                s_pe.inc(te.matmul(ps[PSAB[my_kt % 3]][:, :],
                                                   kbuf[li % 2][:, kt * 128:(kt + 1) * 128],
                                                   q_sb[li % 2][:, :], start=True, stop=True))
                            smm_n = s_pe.n

                            dve_mask_n = None
                            if kt >= 4 * qb:
                                o = kt * 128 - qb * TQ

                                @blk.vector
                                def _(v, my_kt=my_kt, o=o, smm_n=smm_n, mk_n=mk_n):
                                    v.wait_ge(s_pe.h, smm_n)
                                    if my_kt < 8:
                                        v.wait_ge(m_msk.h, mk_n)
                                    s_dve.inc(v.tensor_add(ps[PSAB[my_kt % 3]][:, :],
                                                           ps[PSAB[my_kt % 3]][:, :],
                                                           msk_sb[:, 512 - o: 1024 - o]))
                                dve_mask_n = s_dve.n

                            @blk.scalar
                            def _(sc, my_kt=my_kt, h=h, qb=qb, smm_n=smm_n, dve_mask_n=dve_mask_n):
                                sc.wait_ge(s_pe.h, smm_n)
                                if dve_mask_n is not None:
                                    sc.wait_ge(s_dve.h, dve_mask_n)
                                sl_pt.acquire(sc, my_kt)
                                with nc.allow_low_precision(reason="softmax probs bf16"):
                                    s_act.inc(sc.activation(pt_sb[my_kt % 3][:, :], ps[PSAB[my_kt % 3]][:, :],
                                                            ACTF.Exp, bias=bias_sb[:, h * 8 + qb: h * 8 + qb + 1],
                                                            scale=SCL))
                            exp_n = s_act.n
                            sl_psAB.release(my_kt, s_act)

                            @blk.tensor
                            def _(te, li=li, kt=kt, my_kt=my_kt, h=h, nkt=nkt, exp_n=exp_n):
                                te.wait_ge(s_act.h, exp_n)
                                if kt == 0:
                                    sl_psO.acquire(te, li)
                                    sl_psr.acquire(te, li)
                                te.matmul(ps[2 + li % 2][:, :],
                                          V_sb[:, kt * 1024 + h * 128: kt * 1024 + (h + 1) * 128],
                                          pt_sb[my_kt % 3][:, :],
                                          start=(kt == 0), stop=(kt == nkt - 1))
                                s_pe.inc(te.matmul(ps[4 + li % 2][0:1, :], ones_col[:, :],
                                                   pt_sb[my_kt % 3][:, :],
                                                   start=(kt == 0), stop=(kt == nkt - 1)))
                            sl_pt.release(my_kt, s_pe)
                        omm_n = s_pe.n
                        sl_q.release(li, s_pe)
                        sl_k.release(li, s_pe)

                        if li + 1 < len(iters):
                            p3_loads(blk, li + 1)

                        @blk.vector
                        def _(v, li=li, omm_n=omm_n):
                            v.wait_ge(s_pe.h, omm_n)
                            s_dve.inc(v.tensor_scalar(rg_sb[:, :], ps[4 + li % 2][0:1, :], 1e-38, None,
                                                      ALU.add))
                            v.wait_ge(s_dve.h, s_dve.n)
                            with nc.allow_low_precision(reason="softmax normalizer bf16"):
                                s_dve.inc(v.reciprocal(rb_sb[:, :], rg_sb[:, :]))
                            v.wait_ge(s_dve.h, s_dve.n)
                        rc_n = s_dve.n

                        @blk.tensor
                        def _(te, rc_n=rc_n):
                            te.wait_ge(s_dve.h, rc_n)
                            sl_psB.acquire(te, 0)
                            s_pe.inc(te.matmul(ps[6][:, :], ones_rb[:, :], rb_sb[:, :],
                                               start=True, stop=True))
                        bc_n = s_pe.n

                        @blk.vector
                        def _(v, li=li, h=h, bc_n=bc_n):
                            v.wait_ge(s_pe.h, bc_n)
                            v.tensor_copy(rbc_sb[:, :], ps[6][:, :])
                            if h == 0:
                                sl_attn.acquire(v, 0)
                            with nc.allow_low_precision(reason="attention out bf16"):
                                s_dve.inc(v.tensor_mul(attn_sb[:, h * TQ:(h + 1) * TQ],
                                                       ps[2 + li % 2][:, :], rbc_sb[:, :]))
                        sl_psB.release(0, s_dve)
                        sl_psO.release(li, s_dve)
                        sl_psr.release(li, s_dve)

                        if h == HPC - 1:
                            attn_n = s_dve.n
                            for dtile in range(NCC):
                                wi = qb * NCC + dtile

                                @blk.tensor
                                def _(te, qb=qb, dtile=dtile, wi=wi, attn_n=attn_n, wo_n=wo_n):
                                    if dtile == 0:
                                        te.wait_ge(s_dve.h, attn_n)
                                        if qb == 0:
                                            te.wait_ge(m_wo.h, wo_n)
                                    sl_psAB.acquire(te, wi)
                                    mm = None
                                    for cc in range(HPC):
                                        mm = te.matmul(ps[PSAB[wi % 3]][:, :],
                                                       woT_sb[:, cc * C + dtile * 128: cc * C + (dtile + 1) * 128],
                                                       attn_sb[:, cc * TQ:(cc + 1) * TQ],
                                                       start=(cc == 0), stop=(cc == HPC - 1))
                                    s_pe.inc(mm)
                                wmm_n = s_pe.n

                                @blk.vector
                                def _(v, wi=wi, wmm_n=wmm_n):
                                    v.wait_ge(s_pe.h, wmm_n)
                                    sl_yp.acquire(v, wi)
                                    with nc.allow_low_precision(reason="wo partials bf16 (RS-summed)"):
                                        s_dve.inc(v.tensor_copy(yp_sb[wi % 2][:, :], ps[PSAB[wi % 3]][:, :]))
                                wcp_n = s_dve.n
                                sl_psAB.release(wi, s_dve)

                                @blk.sync
                                def _(sq, qb=qb, dtile=dtile, wi=wi, wcp_n=wcp_n):
                                    sq.wait_ge(s_dve.h, wcp_n)
                                    m_cci[wi % 2].dma(sq.dma_start(
                                        out=d_cci.ap()[qb // 2, dtile * 128:(dtile + 1) * 128,
                                                       (qb % 2) * TQ:((qb % 2) + 1) * TQ],
                                        in_=yp_sb[wi % 2][:, :]))
                                sl_yp.release(wi, m_cci[wi % 2])
                            sl_attn.release(0, s_pe)
                    cci0_n, cci1_n = m_cci[0].n, m_cci[1].n

                    if dbg:
                        pass

        if upto < 5:
            _finish(nc, SEMS)
            return nc

        with ExitStack() as st5:
            xno = st5.enter_context(nc.sbuf_tensor("xno", [128, NCC * TOK], BF16))
            wg_sb = [st5.enter_context(nc.sbuf_tensor(f"wg_sb{i}", [128, NCC * 256], BF16)) for i in range(2)]
            wu_sb = [st5.enter_context(nc.sbuf_tensor(f"wu_sb{i}", [128, NCC * 256], BF16)) for i in range(2)]
            g_sb = [st5.enter_context(nc.sbuf_tensor(f"g_sb{i}", [128, TQ], BF16)) for i in range(2)]
            u_sb = [st5.enter_context(nc.sbuf_tensor(f"u_sb{i}", [128, TQ], BF16)) for i in range(2)]
            mid_sb = [st5.enter_context(nc.sbuf_tensor(f"mid_sb{i}", [128, TQ], BF16)) for i in range(4)]
            sl_wgu = Slot(2)
            sl_psG = seeded(2)
            sl_psU = seeded(2)
            sl_g, sl_u, sl_mid = Slot(2), Slot(2), Slot(4)
            wgu_n = {}

            def load_wgu(blk, hc):
                @blk.gpsimd
                def _(g, hc=hc):
                    sl_wgu.acquire(g, hc)
                    m_wgu[hc % 2].dma(g.dma_start(
                        out=wg_sb[hc % 2][:, :].rearrange("p (a h) -> p a h", a=NCC),
                        in_=d_wg.ap()[hc].transpose([1, 0, 2])))
                    m_wgu[hc % 2].dma(g.dma_start(
                        out=wu_sb[hc % 2][:, :].rearrange("p (a h) -> p a h", a=NCC),
                        in_=d_wu.ap()[hc].transpose([1, 0, 2])))
                wgu_n[hc] = m_wgu[hc % 2].n

            with nc.Block() as blk:
                fence(blk)

                @blk.gpsimd
                def _(g):
                    blksz = NCC * 128 * 128
                    with g.register("rk") as rk, g.register("rk2") as rk2:
                        g.reg_load(rk, idx_sb[:1, :1])
                        g.reg_mul(rk, rk, 8 * blksz)
                        for k in range(8):
                            g.reg_mov(rk2, k * blksz)
                            g.reg_add(rk2, rk2, rk)
                            m_xno.dma(g.dma_start(
                                out=xno[:, k * (NCC * 128):(k + 1) * (NCC * 128)],
                                in_=bass.AP(d_xn, rk2, [[NCC * 128, 128], [1, NCC * 128]])))
                xn_n = m_xno.n

                load_wgu(blk, 0)
                for hc in range(NHC2):
                    for hs in range(2):
                        for th in range(2):
                            i = hc * 4 + hs * 2 + th

                            @blk.tensor
                            def _(te, hc=hc, hs=hs, th=th, i=i, xn_n=xn_n):
                                te.wait_ge(m_wgu[hc % 2].h, wgu_n[hc])
                                if hc == 0:
                                    te.wait_ge(m_xno.h, xn_n)
                                sl_psG.acquire(te, i)
                                sl_psU.acquire(te, i)
                                pG = ps[2 * (i % 2)]
                                pU = ps[2 * (i % 2) + 1]
                                xnr = xno[:, :].rearrange("p (k a t) -> p k a t", k=8, a=NCC)
                                mm = None
                                for cc in range(NCC):
                                    mm = te.matmul(pG[:, :],
                                                   wg_sb[hc % 2][:, cc * 256 + hs * 128: cc * 256 + (hs + 1) * 128],
                                                   xnr[:, 4 * th:4 * th + 4, cc, :],
                                                   start=(cc == 0), stop=(cc == NCC - 1))
                                s_pe.inc(mm)
                                for cc in range(NCC):
                                    mm = te.matmul(pU[:, :],
                                                   wu_sb[hc % 2][:, cc * 256 + hs * 128: cc * 256 + (hs + 1) * 128],
                                                   xnr[:, 4 * th:4 * th + 4, cc, :],
                                                   start=(cc == 0), stop=(cc == NCC - 1))
                                s_pe.inc(mm)
                            g_mm, u_mm = s_pe.n - 1, s_pe.n

                            @blk.scalar
                            def _(sc, i=i, g_mm=g_mm):
                                sc.wait_ge(s_pe.h, g_mm)
                                sl_g.acquire(sc, i)
                                with nc.allow_low_precision(reason="ffn mid bf16"):
                                    s_act.inc(sc.activation(g_sb[i % 2][:, :], ps[2 * (i % 2)][:, :],
                                                            ACTF.Silu, scale=float(sg)))
                            sil_n = s_act.n
                            sl_psG.release(i, s_act)

                            @blk.vector
                            def _(v, i=i, u_mm=u_mm, sil_n=sil_n):
                                v.wait_ge(s_pe.h, u_mm)
                                sl_u.acquire(v, i)
                                with nc.allow_low_precision(reason="ffn mid bf16"):
                                    s_dve.inc(v.tensor_scalar(u_sb[i % 2][:, :], ps[2 * (i % 2) + 1][:, :],
                                                              float(su), None, ALU.mult))
                                    v.wait_ge(s_act.h, sil_n)
                                    v.wait_ge(s_dve.h, s_dve.n)
                                    sl_mid.acquire(v, i)
                                    s_dve.inc(v.tensor_mul(mid_sb[i % 4][:, :], g_sb[i % 2][:, :],
                                                           u_sb[i % 2][:, :]))
                            mid_n = s_dve.n
                            sl_psU.release(i, s_dve)
                            sl_g.release(i, s_dve)
                            sl_u.release(i, s_dve)

                            @blk.sync
                            def _(sq, hc=hc, hs=hs, th=th, i=i, mid_n=mid_n):
                                sq.wait_ge(s_dve.h, mid_n)
                                m_mid[i % 4].dma(sq.dma_start(
                                    out=d_mid.ap()[hc * 2 + hs][:, th * TQ:(th + 1) * TQ],
                                    in_=mid_sb[i % 4][:, :]))
                            sl_mid.release(i, m_mid[i % 4])
                    sl_wgu.release(hc, s_pe)
                    if hc + 1 < NHC2:
                        load_wgu(blk, hc + 1)

                @blk.gpsimd
                def _(g):
                    g.wait_ge(m_cci[0].h, cci0_n)
                    g.wait_ge(m_cci[1].h, cci1_n)
                    s_cc.inc(g.collective_compute(
                        "ReduceScatter", ALU.add,
                        replica_groups=[[0, 1, 2, 3], [4, 5, 6, 7]],
                        ins=[d_cci.ap()], outs=[d_cco.ap()]))
                if dbg:
                    @blk.gpsimd
                    def _(g):
                        g.wait_ge(s_cc.h, 1)
                        m_dbg.dma(g.dma_start(out=d_ccid.ap(), in_=d_cci.ap()))
                        m_dbg.dma(g.dma_start(out=d_ccod.ap(), in_=d_cco.ap()))
            mid_totals = [(m.h, m.n) for m in m_mid]

        if upto < 6:
            _finish(nc, SEMS)
            return nc

        with ExitStack() as st6:
            mid6 = st6.enter_context(nc.sbuf_tensor("mid6", [128, NHC2 * TOK], BF16))
            yh_sb = st6.enter_context(nc.sbuf_tensor("yh_sb", [128, NCC * TOK], BF16))
            wd_sb = [st6.enter_context(nc.sbuf_tensor(f"wd_sb{i}", [128, NHC2 * 128], BF16)) for i in range(2)]
            xt_sb = [st6.enter_context(nc.sbuf_tensor(f"xt_sb{i}", [128, TQ], F32)) for i in range(2)]
            ya_sb = [st6.enter_context(nc.sbuf_tensor(f"ya_sb{i}", [128, TQ], BF16)) for i in range(2)]
            tm_sb = [st6.enter_context(nc.sbuf_tensor(f"tm_sb{i}", [128, TQ], F32)) for i in range(2)]
            tn_sb = [st6.enter_context(nc.sbuf_tensor(f"tn_sb{i}", [128, TQ], F32)) for i in range(2)]
            y_sb = [st6.enter_context(nc.sbuf_tensor(f"y_sb{i}", [128, TQ], F32)) for i in range(2)]
            sl_mid6 = Slot(1)
            sl_wd = Slot(2)
            sl_psY = seeded(2)
            sl_xt, sl_ya, sl_y = Slot(2), Slot(2), Slot(2)
            wd_n = {}

            def load_wd(blk, half, dtile):
                @blk.gpsimd
                def _(g, half=half, dtile=dtile):
                    sl_wd.acquire(g, half * NCC + dtile)
                    m_wd[dtile % 2].dma(g.dma_start(
                        out=wd_sb[dtile % 2][:, :].rearrange("p (a d) -> p a d", a=NHC2),
                        in_=d_wd.ap()[dtile, half * NHC2:(half + 1) * NHC2, :, :].transpose([1, 0, 2])))
                wd_n[(half, dtile)] = m_wd[dtile % 2].n

            with nc.Block() as blk:
                fence(blk)
                load_wd(blk, 0, 0)
                for half in range(2):
                    @blk.sync
                    def _(sq, half=half):
                        if half == 0:
                            for hh, vv in mid_totals:
                                sq.wait_ge(hh, vv)
                        sl_mid6.acquire(sq, 0)
                        m_m6.dma(sq.dma_start(
                            out=mid6[:, :].rearrange("p (a t) -> p a t", a=NHC2),
                            in_=d_mid.ap()[half * NHC2:(half + 1) * NHC2, :, :].transpose([1, 0, 2])))
                    mid_ld = m_m6.n

                    for dtile in range(NCC):
                        for th in range(2):
                            i = (half * NCC + dtile) * 2 + th

                            @blk.tensor
                            def _(te, half=half, dtile=dtile, th=th, i=i, mid_ld=mid_ld):
                                te.wait_ge(m_wd[dtile % 2].h, wd_n[(half, dtile)])
                                if dtile == 0 and th == 0:
                                    te.wait_ge(m_m6.h, mid_ld)
                                sl_psY.acquire(te, i)
                                pY = ps[i % 2]
                                mm = None
                                for hcc in range(NHC2):
                                    mm = te.matmul(pY[:, :],
                                                   wd_sb[dtile % 2][:, hcc * 128:(hcc + 1) * 128],
                                                   mid6[:, hcc * TOK + th * TQ: hcc * TOK + (th + 1) * TQ],
                                                   start=(hcc == 0), stop=(hcc == NHC2 - 1))
                                s_pe.inc(mm)
                            y_mm = s_pe.n

                            if half == 0:
                                @blk.vector
                                def _(v, dtile=dtile, th=th, i=i, y_mm=y_mm):
                                    v.wait_ge(s_pe.h, y_mm)
                                    with nc.allow_low_precision(reason="half-sum bf16 staging"):
                                        s_dve.inc(v.tensor_copy(
                                            yh_sb[:, dtile * TOK + th * TQ: dtile * TOK + (th + 1) * TQ],
                                            ps[i % 2][:, :]))
                                sl_psY.release(i, s_dve)
                            else:
                                @blk.sync
                                def _(sq, dtile=dtile, th=th, i=i):
                                    sl_xt.acquire(sq, i)
                                    sl_ya.acquire(sq, i)
                                    if dtile == 0 and th == 0:
                                        sq.wait_ge(s_cc.h, 1)
                                    m_ep[i % 2].dma(sq.dma_start(
                                        out=xt_sb[i % 2][:, :],
                                        in_=d_xT[dtile * 128:(dtile + 1) * 128, th * TQ:(th + 1) * TQ]))
                                    m_ep[i % 2].dma(sq.dma_start(
                                        out=ya_sb[i % 2][:, :],
                                        in_=d_cco.ap()[dtile * 128:(dtile + 1) * 128,
                                                       th * TQ:(th + 1) * TQ]))
                                ep_ld = m_ep[i % 2].n

                                @blk.vector
                                def _(v, dtile=dtile, th=th, i=i, y_mm=y_mm, ep_ld=ep_ld):
                                    v.wait_ge(s_pe.h, y_mm)
                                    v.wait_ge(m_ep[i % 2].h, ep_ld)
                                    v.tensor_add(tm_sb[i % 2][:, :], ps[i % 2][:, :],
                                                 yh_sb[:, dtile * TOK + th * TQ: dtile * TOK + (th + 1) * TQ])
                                    v.scalar_tensor_tensor(tn_sb[i % 2][:, :], tm_sb[i % 2][:, :],
                                                           float(sd), ya_sb[i % 2][:, :], ALU.mult, ALU.add)
                                    sl_y.acquire(v, i)
                                    s_dve.inc(v.tensor_add(y_sb[i % 2][:, :], tn_sb[i % 2][:, :],
                                                           xt_sb[i % 2][:, :]))
                                yv_n = s_dve.n
                                sl_psY.release(i, s_dve)
                                sl_xt.release(i, s_dve)
                                sl_ya.release(i, s_dve)

                                @blk.sync
                                def _(sq, dtile=dtile, th=th, i=i, yv_n=yv_n):
                                    sq.wait_ge(s_dve.h, yv_n)
                                    m_y[i % 2].dma(sq.dma_start(
                                        out=d_y[dtile * 128:(dtile + 1) * 128, th * TQ:(th + 1) * TQ],
                                        in_=y_sb[i % 2][:, :]))
                                sl_y.release(i, m_y[i % 2])
                        sl_wd.release(half * NCC + dtile, s_pe)
                        nxt = (half, dtile + 1) if dtile + 1 < NCC else ((1, 0) if half == 0 else None)
                        if nxt is not None:
                            load_wd(blk, *nxt)
                    sl_mid6.release(0, s_pe)

        _finish(nc, SEMS)
    return nc


def _finish(nc, SEMS):
    with nc.Block() as blk:
        vals = [(s.h, s.n) for s in SEMS if s.n > 0]

        @blk.sync
        def _(sq):
            for h, v in vals:
                sq.wait_ge(h, v)

        @blk.gpsimd
        def _(g):
            for h, v in vals:
                g.wait_ge(h, v)


def _host_prep(inputs):
    x = np.asarray(inputs["L0_attn_x"], dtype=np.float32)
    wq = np.asarray(inputs["L0_attn_wq"], dtype=np.float32)
    wk = np.asarray(inputs["L0_attn_wk"], dtype=np.float32)
    wv = np.asarray(inputs["L0_attn_wv"], dtype=np.float32)
    wo = np.asarray(inputs["L0_attn_wo"], dtype=np.float32)
    wg = np.asarray(inputs["L0_ffn_w_gate"], dtype=np.int32)
    wu = np.asarray(inputs["L0_ffn_w_up"], dtype=np.int32)
    wd = np.asarray(inputs["L0_ffn_w_down"], dtype=np.int32)

    wqT = np.ascontiguousarray(wq.T)
    wkT = np.ascontiguousarray(wk.T)
    wvT = np.ascontiguousarray(wv.T)
    woT = np.ascontiguousarray(wo.T)
    wg_t = np.ascontiguousarray(wg.T.reshape(NCC, 128, NHC2, 256).transpose(2, 0, 1, 3))
    wu_t = np.ascontiguousarray(wu.T.reshape(NCC, 128, NHC2, 256).transpose(2, 0, 1, 3))
    wd_t = np.ascontiguousarray(wd.T.reshape(NHC, 128, NCC, 128).transpose(2, 0, 1, 3))

    pos = np.arange(T, dtype=np.float64)
    thetas = 1.0 / 10000.0 ** (np.arange(0, HD, 2, dtype=np.float64) / HD)
    pt = np.concatenate([np.outer(pos, thetas)] * 2, axis=1)
    sin_t = np.ascontiguousarray(np.sin(pt).T.astype(np.float32))
    cos_t = np.ascontiguousarray(np.cos(pt).T.astype(np.float32))

    rr = np.zeros((HD, HD), dtype=np.float32)
    for i in range(HD // 2):
        rr[i, i + HD // 2] = -1.0
        rr[i + HD // 2, i] = 1.0
    rt = np.ascontiguousarray(rr.T).astype(ml_dtypes.bfloat16)

    ii, jj = np.meshgrid(np.arange(128), np.arange(1024), indexing="ij")
    msk = np.ascontiguousarray(np.where(jj < ii + 512, -1e9, 0.0).astype(np.float32))

    x_b = [np.ascontiguousarray(x[b]) for b in range(B)]
    xT_b = [np.ascontiguousarray(x[b].T) for b in range(B)]

    in_maps = []
    for core in range(8):
        b, j = core // 4, core % 4
        m = {
            "x_tok": x_b[b],
            "xT_own": np.ascontiguousarray(xT_b[b][:, TOK * j: TOK * (j + 1)]),
            "wvT_g": np.ascontiguousarray(wvT[:, HPC * HD * j: HPC * HD * (j + 1)]),
            "woT_mine": np.ascontiguousarray(woT[HPC * HD * j: HPC * HD * (j + 1), :]),
            "wg_t": wg_t, "wu_t": wu_t, "wd_t": wd_t,
            "sin_t": sin_t, "cos_t": cos_t, "rope_rt": rt, "msk": msk,
            "quarter_idx": np.array([[j]], dtype=np.int32),
        }
        for p in range(2):
            d0 = HD * (HPC * j + HPP * p)
            m[f"wkT_p{p}"] = np.ascontiguousarray(wkT[:, d0:d0 + HPP * HD])
            m[f"wqT_p{p}"] = np.ascontiguousarray(wqT[:, d0:d0 + HPP * HD])
        in_maps.append(m)
    return in_maps


def kernel(**inputs):
    sg = float(np.asarray(inputs["L0_ffn_scale_gate"]))
    su = float(np.asarray(inputs["L0_ffn_scale_up"]))
    sd = float(np.asarray(inputs["L0_ffn_scale_down"]))
    in_maps = _host_prep(inputs)
    nc = build(sg, su, sd, upto=_UPTO, dbg=_DBG_OUT)
    res = run_bass_kernel_spmd(nc, in_maps, core_ids=list(range(8)))
    out = np.empty((B, T, C), dtype=np.float32)
    for core in range(8):
        b, j = core // 4, core % 4
        y = np.asarray(res.results[core]["y_out"])
        out[b, TOK * j: TOK * (j + 1), :] = y.T
    if _DBG_OUT:
        kernel._last_results = res
    return out


# version 03 (final)
# speedup vs baseline: 1.1763x; 1.1763x over previous
import numpy as np
import ml_dtypes
from contextlib import ExitStack

import concourse.bass as bass
import concourse.mybir as mybir
from concourse.bass_utils import run_bass_kernel_spmd

F32 = mybir.dt.float32
BF16 = mybir.dt.bfloat16
I32 = mybir.dt.int32
ALU = mybir.AluOpType
ACTF = mybir.ActivationFunctionType
AX = mybir.AxisListType

B, T, C = 2, 4096, 4096
NH, HD = 32, 128
HPC = 8
HPP = 4
TQ = 512
NT = T // TQ
NST = T // 128
NCC = C // 128
TOK = 1024
HID = 11008
NHC = 86
NHC2 = 43
SCL = 1.0 / float(np.sqrt(128.0))
BND_PAD = 1.02
EPS = 1e-5

_UPTO = 6
_DBG_OUT = False


class Sem:
    __slots__ = ("h", "n")

    def __init__(self, stack, nc, name, reg):
        self.h = stack.enter_context(nc.semaphore(name))
        self.n = 0
        reg.append(self)

    def inc(self, inst, k=1):
        inst.then_inc(self.h, k)
        self.n += k

    def dma(self, inst):
        inst.then_inc(self.h, 16)
        self.n += 16


class Slot:

    def __init__(self, k, seed=None):
        self.g = [list(seed or []) for _ in range(k)]
        self.k = k

    def acquire(self, eng, i):
        for (s, v) in self.g[i % self.k]:
            eng.wait_ge(s.h, v)
        self.g[i % self.k] = []

    def release(self, i, sem):
        self.g[i % self.k].append((sem, sem.n))


def build(sg, su, sd, upto=6, dbg=False):
    nc = bass.Bass("TRN2", target_bir_lowering=False)

    d_x = nc.dram_tensor("x_tok", [T, C], F32, kind="ExternalInput")
    d_xT = nc.dram_tensor("xT_own", [C, TOK], F32, kind="ExternalInput")
    d_wk = [nc.dram_tensor(f"wkT_p{p}", [C, HPP * HD], F32, kind="ExternalInput") for p in range(2)]
    d_wq = [nc.dram_tensor(f"wqT_p{p}", [C, HPP * HD], F32, kind="ExternalInput") for p in range(2)]
    d_wv = nc.dram_tensor("wvT_g", [C, HPC * HD], F32, kind="ExternalInput")
    d_wo = nc.dram_tensor("woT_mine", [HPC * HD, C], F32, kind="ExternalInput")
    d_wg = nc.dram_tensor("wg_t", [NHC2, NCC, 128, 256], I32, kind="ExternalInput")
    d_wu = nc.dram_tensor("wu_t", [NHC2, NCC, 128, 256], I32, kind="ExternalInput")
    d_wd = nc.dram_tensor("wd_t", [NCC, NHC, 128, 128], I32, kind="ExternalInput")
    d_sin = nc.dram_tensor("sin_t", [HD, T], F32, kind="ExternalInput")
    d_cos = nc.dram_tensor("cos_t", [HD, T], F32, kind="ExternalInput")
    d_rt = nc.dram_tensor("rope_rt", [HD, HD], BF16, kind="ExternalInput")
    d_msk = nc.dram_tensor("msk", [128, 1024], F32, kind="ExternalInput")
    d_idx = nc.dram_tensor("quarter_idx", [1, 1], I32, kind="ExternalInput")

    d_y = nc.dram_tensor("y_out", [C, TOK], F32, kind="ExternalOutput")
    sk = dict(kind="ExternalOutput") if dbg else {}
    d_xn = nc.dram_tensor("xn_sc", [NST, 128, NCC * 128], BF16, **sk)
    d_qs = nc.dram_tensor("q_sc", [HPC, HD, T], BF16, **sk)
    d_ks = nc.dram_tensor("k_sc", [HPC, HD, T], BF16, **sk)
    d_mid = nc.dram_tensor("mid_sc", [NHC, 128, TOK], BF16, **sk)
    d_cci = nc.dram_tensor("cc_in", [4, C, TOK], BF16)
    d_cco = nc.dram_tensor("cc_out", [C, TOK], BF16)
    if dbg:
        d_vdb = nc.dram_tensor("v_dbg", [128, NST * HPC * HD], BF16, kind="ExternalOutput")
        d_ccid = nc.dram_tensor("cci_dbg", [4, C, TOK], BF16, kind="ExternalOutput")
        d_ccod = nc.dram_tensor("cco_dbg", [C, TOK], BF16, kind="ExternalOutput")
        d_bdb = nc.dram_tensor("b_dbg", [128, 64], F32, kind="ExternalOutput")

    gs = ExitStack()
    with gs:
        ps = [gs.enter_context(nc.psum_tensor(f"ps{i}", [128, 512], F32)) for i in range(8)]

        SEMS = []
        s_pe = Sem(gs, nc, "s_pe", SEMS)
        s_dve = Sem(gs, nc, "s_dve", SEMS)
        s_act = Sem(gs, nc, "s_act", SEMS)
        s_cc = Sem(gs, nc, "s_cc", [])
        m_init = Sem(gs, nc, "m_init", SEMS)
        m_x = [Sem(gs, nc, f"m_x{i}", SEMS) for i in range(2)]
        m_xb = [Sem(gs, nc, f"m_xb{i}", SEMS) for i in range(2)]
        m_st = [Sem(gs, nc, f"m_st{i}", SEMS) for i in range(2)]
        m_wv = Sem(gs, nc, "m_wv", SEMS)
        m_xnt = Sem(gs, nc, "m_xnt", SEMS)
        m_rt = Sem(gs, nc, "m_rt", SEMS)
        m_sc = [Sem(gs, nc, f"m_sc{i}", SEMS) for i in range(2)]
        m_wkq = Sem(gs, nc, "m_wkq", SEMS)
        m_kst = [Sem(gs, nc, f"m_kst{i}", SEMS) for i in range(2)]
        m_wo = Sem(gs, nc, "m_wo", SEMS)
        m_msk = Sem(gs, nc, "m_msk", SEMS)
        m_q = [Sem(gs, nc, f"m_q{i}", SEMS) for i in range(2)]
        m_k = [Sem(gs, nc, f"m_k{i}", SEMS) for i in range(2)]
        m_cci = [Sem(gs, nc, f"m_cci{i}", SEMS) for i in range(2)]
        m_xno = Sem(gs, nc, "m_xno", SEMS)
        m_wgu = [Sem(gs, nc, f"m_wgu{i}", SEMS) for i in range(2)]
        m_mid = [Sem(gs, nc, f"m_mid{i}", SEMS) for i in range(4)]
        m_wd = [Sem(gs, nc, f"m_wd{i}", SEMS) for i in range(2)]
        m_m6 = Sem(gs, nc, "m_m6", SEMS)
        m_ep = [Sem(gs, nc, f"m_ep{i}", SEMS) for i in range(2)]
        m_y = [Sem(gs, nc, f"m_y{i}", SEMS) for i in range(2)]
        m_dbg = Sem(gs, nc, "m_dbg", SEMS) if dbg else None

        def fence(blk):
            vals = [(s.h, s.n) for s in SEMS if s.n > 0]

            @blk.sync
            def _(sq):
                for h, v in vals:
                    sq.wait_ge(h, v)

            @blk.scalar
            def _(sc):
                for h, v in vals:
                    sc.wait_ge(h, v)

            @blk.gpsimd
            def _(g):
                for h, v in vals:
                    g.wait_ge(h, v)

        def seeded(k):
            return Slot(k, seed=[(s_dve, s_dve.n), (s_act, s_act.n), (s_pe, s_pe.n)])

        ones_col = gs.enter_context(nc.sbuf_tensor("ones_col", [128, 1], BF16))
        ones_rb = gs.enter_context(nc.sbuf_tensor("ones_rb", [1, 128], BF16))
        ones_rf = gs.enter_context(nc.sbuf_tensor("ones_rf", [1, 128], F32))
        eps_sb = gs.enter_context(nc.sbuf_tensor("eps_sb", [128, 1], F32))
        bias_sb = gs.enter_context(nc.sbuf_tensor("bias_sb", [128, 64], F32))
        bq_sb = gs.enter_context(nc.sbuf_tensor("bq_sb", [1, 64], F32))
        bk_sb = gs.enter_context(nc.sbuf_tensor("bk_sb", [1, 8], F32))
        idx_sb = gs.enter_context(nc.sbuf_tensor("idx_sb", [1, 1], I32))
        t0a = gs.enter_context(nc.sbuf_tensor("t0a", [1, 1], F32))
        t0b = gs.enter_context(nc.sbuf_tensor("t0b", [1, 1], F32))
        sumx = gs.enter_context(nc.sbuf_tensor("sumx", [128, 1], F32))
        sumsq = gs.enter_context(nc.sbuf_tensor("sumsq", [128, 1], F32))
        negm = gs.enter_context(nc.sbuf_tensor("negm", [128, 1], F32))
        varr = gs.enter_context(nc.sbuf_tensor("varr", [128, 1], F32))
        m2t = gs.enter_context(nc.sbuf_tensor("m2t", [128, 1], F32))
        sqv = gs.enter_context(nc.sbuf_tensor("sqv", [128, 1], F32))
        rinv = gs.enter_context(nc.sbuf_tensor("rinv", [128, 1], F32))

        with nc.Block() as blk:
            @blk.vector
            def _(v):
                v.memset(ones_col[:, :], 1.0)
                v.memset(ones_rb[:, :], 1.0)
                v.memset(ones_rf[:, :], 1.0)
                v.memset(bq_sb[:, :], 0.0)
                v.memset(bk_sb[:, :], 0.0)
                s_dve.inc(v.memset(eps_sb[:, :], EPS))

            @blk.gpsimd
            def _(g):
                m_init.dma(g.dma_start(out=idx_sb[:, :], in_=d_idx[:, :]))

        with ExitStack() as st1:
            x_sb = [st1.enter_context(nc.sbuf_tensor(f"x_sb{i}", [128, C], F32)) for i in range(2)]
            junk = st1.enter_context(nc.sbuf_tensor("junk", [128, C], BF16))
            xn_sb = [st1.enter_context(nc.sbuf_tensor(f"xn_sb{i}", [128, C], BF16)) for i in range(2)]
            xnf_sb = [st1.enter_context(nc.sbuf_tensor(f"xnf_sb{i}", [128, C], BF16)) for i in range(2)]
            sl_x, sl_xn, sl_xnf = Slot(2), Slot(2), Slot(2)
            dve_stats = {}

            with nc.Block() as blk:
                for t8 in range(NST):
                    @blk.sync
                    def _(sq, t8=t8):
                        sl_x.acquire(sq, t8)
                        m_x[t8 % 2].dma(sq.dma_start(out=x_sb[t8 % 2][:, :],
                                                     in_=d_x[t8 * 128:(t8 + 1) * 128, :]))
                    ldx_n = m_x[t8 % 2].n

                    @blk.scalar
                    def _(sc, t8=t8, ldx_n=ldx_n):
                        sc.wait_ge(m_x[t8 % 2].h, ldx_n)
                        if t8 == 0:
                            sc.wait_ge(s_dve.h, 1)
                        else:
                            sc.wait_ge(s_dve.h, dve_stats[t8 - 1])
                        s_act.inc(sc.activation(junk[:, :], x_sb[t8 % 2][:, :], ACTF.Square,
                                                accum_out=sumsq[:, :]))
                        s_act.inc(sc.activation(junk[:, :], x_sb[t8 % 2][:, :], ACTF.Copy,
                                                accum_out=sumx[:, :]))
                    acc_n = s_act.n

                    @blk.vector
                    def _(v, acc_n=acc_n):
                        v.wait_ge(s_act.h, acc_n)
                        v.tensor_scalar(negm[:, :], sumx[:, :], -1.0 / C, None, ALU.mult)
                        s_dve.inc(v.tensor_scalar(varr[:, :], sumsq[:, :], 1.0 / C, None, ALU.mult))
                        v.wait_ge(s_dve.h, s_dve.n)
                        s_dve.inc(v.tensor_mul(m2t[:, :], negm[:, :], negm[:, :]))
                        v.wait_ge(s_dve.h, s_dve.n)
                        s_dve.inc(v.tensor_sub(varr[:, :], varr[:, :], m2t[:, :]))
                    var_n = s_dve.n

                    @blk.scalar
                    def _(sc, var_n=var_n):
                        sc.wait_ge(s_dve.h, var_n)
                        s_act.inc(sc.activation(sqv[:, :], varr[:, :], ACTF.Sqrt, bias=eps_sb[:, :]))
                    sq_n = s_act.n

                    @blk.vector
                    def _(v, t8=t8, sq_n=sq_n):
                        v.wait_ge(s_act.h, sq_n)
                        s_dve.inc(v.reciprocal(rinv[:, :], sqv[:, :]))
                        v.wait_ge(s_dve.h, s_dve.n)
                        sl_xn.acquire(v, t8)
                        with nc.allow_low_precision(reason="bf16 matmul operand stream"):
                            s_dve.inc(v.tensor_scalar(xn_sb[t8 % 2][:, :], x_sb[t8 % 2][:, :],
                                                      negm[:, :], rinv[:, :], ALU.add, ALU.mult))
                    xn_n = s_dve.n
                    dve_stats[t8] = xn_n
                    sl_x.release(t8, s_act)
                    sl_x.release(t8, s_dve)

                    @blk.scalar
                    def _(sc, t8=t8, xn_n=xn_n):
                        sc.wait_ge(s_dve.h, xn_n)
                        sl_xnf.acquire(sc, t8)
                        for cc in range(NCC):
                            m_xb[t8 % 2].dma(sc.dma_start(out=xnf_sb[t8 % 2][:, cc * 128:(cc + 1) * 128],
                                                          in_=xn_sb[t8 % 2][:, cc * 128:(cc + 1) * 128],
                                                          transpose=True))
                    xba_n = m_xb[t8 % 2].n

                    @blk.sync
                    def _(sq, t8=t8, xba_n=xba_n):
                        sq.wait_ge(m_xb[t8 % 2].h, xba_n)
                        m_st[t8 % 2].dma(sq.dma_start(
                            out=d_xn.ap()[t8],
                            in_=xnf_sb[t8 % 2][:, :]))
                    sl_xn.release(t8, m_xb[t8 % 2])
                    sl_xnf.release(t8, m_st[t8 % 2])

        if upto < 2:
            _finish(nc, SEMS)
            return nc

        with ExitStack() as stA:
            V_sb = stA.enter_context(nc.sbuf_tensor("V_sb", [128, NST * HPC * HD], BF16))
            xnb = stA.enter_context(nc.sbuf_tensor("xnb", [128, NCC * TQ], BF16))
            sl_xnb = Slot(1)

            def load_xn_tile(sq, tt):
                for k in range(4):
                    t8 = 4 * tt + k
                    m_xnt.dma(sq.dma_start(
                        out=xnb[:, k * (NCC * 128):(k + 1) * (NCC * 128)],
                        in_=d_xn.ap()[t8]))

            with ExitStack() as stV:
                wv_sb = stV.enter_context(nc.sbuf_tensor("wv_sb", [128, NCC * HPC * HD], BF16))
                sl_psV = seeded(2)

                with nc.Block() as blk:
                    fence(blk)

                    @blk.gpsimd
                    def _(g):
                        m_wv.dma(g.dma_start(out=wv_sb[:, :].rearrange("p (a d) -> p a d", a=NCC),
                                             in_=d_wv.ap().rearrange("(a p) d -> p a d", p=128)))
                    wv_n = m_wv.n

                    for tt in range(NT):
                        @blk.sync
                        def _(sq, tt=tt):
                            sl_xnb.acquire(sq, 0)
                            load_xn_tile(sq, tt)
                        ld_n = m_xnt.n

                        for ts4 in range(4):
                            i = tt * 4 + ts4

                            @blk.tensor
                            def _(te, tt=tt, ts4=ts4, i=i, ld_n=ld_n):
                                te.wait_ge(m_xnt.h, ld_n)
                                if i == 0:
                                    te.wait_ge(m_wv.h, wv_n)
                                sl_psV.acquire(te, i)
                                pA, pB = ps[2 * (i % 2)], ps[2 * (i % 2) + 1]
                                mm = None
                                for cc in range(NCC):
                                    for dh, pX in ((0, pA), (1, pB)):
                                        mm = te.matmul(pX[:, :],
                                                       xnb[:, ts4 * 4096 + cc * 128: ts4 * 4096 + (cc + 1) * 128],
                                                       wv_sb[:, cc * 1024 + dh * 512: cc * 1024 + (dh + 1) * 512],
                                                       start=(cc == 0), stop=(cc == NCC - 1))
                                s_pe.inc(mm)
                            mm_n = s_pe.n

                            @blk.vector
                            def _(v, tt=tt, ts4=ts4, i=i, mm_n=mm_n):
                                v.wait_ge(s_pe.h, mm_n)
                                pA, pB = ps[2 * (i % 2)], ps[2 * (i % 2) + 1]
                                t8 = tt * 4 + ts4
                                with nc.allow_low_precision(reason="V operand bf16"):
                                    v.tensor_copy(V_sb[:, t8 * 1024: t8 * 1024 + 512], pA[:, :])
                                    s_dve.inc(v.tensor_copy(V_sb[:, t8 * 1024 + 512: t8 * 1024 + 1024],
                                                            pB[:, :]))
                            sl_psV.release(i, s_dve)
                        sl_xnb.release(0, s_pe)
                    if dbg:
                        @blk.sync
                        def _(sq):
                            sq.wait_ge(s_dve.h, s_dve.n)
                            m_dbg.dma(sq.dma_start(out=d_vdb[:, :], in_=V_sb[:, :]))

            with ExitStack() as stK:
                wk_sb = stK.enter_context(nc.sbuf_tensor("wk_sb", [128, NCC * HPP * HD], BF16))
                wq_sb = stK.enter_context(nc.sbuf_tensor("wq_sb", [128, NCC * HPP * HD], BF16))
                rt_sb = stK.enter_context(nc.sbuf_tensor("rt_sb", [HD, HD], BF16))
                sin_sb = [stK.enter_context(nc.sbuf_tensor(f"sin_sb{i}", [HD, TQ], F32)) for i in range(2)]
                cos_sb = [stK.enter_context(nc.sbuf_tensor(f"cos_sb{i}", [HD, TQ], F32)) for i in range(2)]
                ks_sb = [stK.enter_context(nc.sbuf_tensor(f"ks_sb{i}", [128, TQ], BF16)) for i in range(2)]
                kc_sb = [stK.enter_context(nc.sbuf_tensor(f"kc_sb{i}", [128, TQ], BF16)) for i in range(2)]
                kf_sb = [stK.enter_context(nc.sbuf_tensor(f"kf_sb{i}", [128, TQ], BF16)) for i in range(2)]
                k2_sb = stK.enter_context(nc.sbuf_tensor("k2_sb", [128, TQ], BF16))
                sl_w = Slot(1)
                sl_k2 = Slot(1)
                sl_sct = Slot(2)
                sl_psK = seeded(2)
                sl_psR = seeded(2)
                sl_psN = seeded(1)
                sl_ks = Slot(2)
                sl_kf = Slot(2)

                with nc.Block() as blk:
                    fence(blk)

                    @blk.sync
                    def _(sq):
                        m_rt.dma(sq.dma_start(out=rt_sb[:, :], in_=d_rt[:, :]))
                    rt_n = m_rt.n

                    ri = 0
                    for p in range(2):
                        @blk.gpsimd
                        def _(g, p=p):
                            sl_w.acquire(g, 0)
                            m_wkq.dma(g.dma_start(out=wk_sb[:, :].rearrange("p (a d) -> p a d", a=NCC),
                                                  in_=d_wk[p].ap().rearrange("(a p) d -> p a d", p=128)))
                            m_wkq.dma(g.dma_start(out=wq_sb[:, :].rearrange("p (a d) -> p a d", a=NCC),
                                                  in_=d_wq[p].ap().rearrange("(a p) d -> p a d", p=128)))
                        w_n = m_wkq.n

                        for tt in range(NT):
                            @blk.sync
                            def _(sq, tt=tt):
                                sl_xnb.acquire(sq, 0)
                                load_xn_tile(sq, tt)
                                sl_sct.acquire(sq, tt)
                                m_sc[tt % 2].dma(sq.dma_start(out=sin_sb[tt % 2][:, :],
                                                              in_=d_sin[:, tt * TQ:(tt + 1) * TQ]))
                                m_sc[tt % 2].dma(sq.dma_start(out=cos_sb[tt % 2][:, :],
                                                              in_=d_cos[:, tt * TQ:(tt + 1) * TQ]))
                            ld_n = m_xnt.n
                            sc_n = m_sc[tt % 2].n

                            for proj, w_t, d_out, b_t in (("k", wk_sb, d_ks, bk_sb),
                                                          ("q", wq_sb, d_qs, bq_sb)):
                                for dt in range(HPP):
                                    h = HPP * p + dt
                                    my = ri
                                    ri += 1

                                    @blk.tensor
                                    def _(te, w_t=w_t, dt=dt, my=my, ld_n=ld_n, w_n=w_n):
                                        te.wait_ge(m_xnt.h, ld_n)
                                        te.wait_ge(m_wkq.h, w_n)
                                        sl_psK.acquire(te, my)
                                        pK = ps[my % 2]
                                        xnr = xnb[:, :].rearrange("p (k a t) -> p k a t", k=4, a=NCC)
                                        mm = None
                                        for cc in range(NCC):
                                            mm = te.matmul(pK[:, :],
                                                           w_t[:, cc * (HPP * HD) + dt * 128:
                                                               cc * (HPP * HD) + (dt + 1) * 128],
                                                           xnr[:, :, cc, :],
                                                           start=(cc == 0), stop=(cc == NCC - 1))
                                        s_pe.inc(mm)
                                    mm_n = s_pe.n

                                    @blk.vector
                                    def _(v, tt=tt, my=my, mm_n=mm_n, sc_n=sc_n):
                                        v.wait_ge(s_pe.h, mm_n)
                                        v.wait_ge(m_sc[tt % 2].h, sc_n)
                                        sl_ks.acquire(v, my)
                                        pK = ps[my % 2]
                                        with nc.allow_low_precision(reason="rope bf16"):
                                            v.tensor_mul(ks_sb[my % 2][:, :], pK[:, :], sin_sb[tt % 2][:, :])
                                            s_dve.inc(v.tensor_mul(kc_sb[my % 2][:, :], pK[:, :],
                                                                   cos_sb[tt % 2][:, :]))
                                    mul_n = s_dve.n
                                    sl_psK.release(my, s_dve)

                                    @blk.tensor
                                    def _(te, my=my, mul_n=mul_n, rt_n=rt_n):
                                        te.wait_ge(s_dve.h, mul_n)
                                        if my == 0:
                                            te.wait_ge(m_rt.h, rt_n)
                                        sl_psR.acquire(te, my)
                                        s_pe.inc(te.matmul(ps[2 + my % 2][:, :], rt_sb[:, :],
                                                           ks_sb[my % 2][:, :], start=True, stop=True))
                                    rmm_n = s_pe.n

                                    @blk.vector
                                    def _(v, my=my, rmm_n=rmm_n):
                                        v.wait_ge(s_pe.h, rmm_n)
                                        sl_kf.acquire(v, my)
                                        with nc.allow_low_precision(reason="rope bf16"):
                                            s_dve.inc(v.tensor_add(kf_sb[my % 2][:, :], ps[2 + my % 2][:, :],
                                                                   kc_sb[my % 2][:, :]))
                                    kf_n = s_dve.n
                                    sl_ks.release(my, s_dve)
                                    sl_psR.release(my, s_dve)

                                    @blk.scalar
                                    def _(sc, my=my, kf_n=kf_n):
                                        sc.wait_ge(s_dve.h, kf_n)
                                        sl_k2.acquire(sc, 0)
                                        s_act.inc(sc.activation(k2_sb[:, :], kf_sb[my % 2][:, :], ACTF.Square))
                                    k2_n = s_act.n

                                    @blk.tensor
                                    def _(te, k2_n=k2_n):
                                        te.wait_ge(s_act.h, k2_n)
                                        sl_psN.acquire(te, 0)
                                        s_pe.inc(te.matmul(ps[4][0:1, :], ones_col[:, :], k2_sb[:, :],
                                                           start=True, stop=True))
                                    nrm_n = s_pe.n
                                    sl_k2.release(0, s_pe)

                                    @blk.vector
                                    def _(v, h=h, tt=tt, proj=proj, b_t=b_t, nrm_n=nrm_n):
                                        v.wait_ge(s_pe.h, nrm_n)
                                        if proj == "q":
                                            s_dve.inc(v.reduce_max(bq_sb[0:1, h * 8 + tt: h * 8 + tt + 1],
                                                                   ps[4][0:1, :], axis=AX.X))
                                            v.wait_ge(s_dve.h, s_dve.n)
                                        else:
                                            s_dve.inc(v.reduce_max(t0a[:, :], ps[4][0:1, :], axis=AX.X))
                                            v.wait_ge(s_dve.h, s_dve.n)
                                            s_dve.inc(v.tensor_max(b_t[0:1, h:h + 1], b_t[0:1, h:h + 1],
                                                                   t0a[:, :]))
                                            v.wait_ge(s_dve.h, s_dve.n)
                                    sl_psN.release(0, s_dve)

                                    @blk.sync
                                    def _(sq, h=h, tt=tt, my=my, d_out=d_out, kf_n=kf_n):
                                        sq.wait_ge(s_dve.h, kf_n)
                                        m_kst[my % 2].dma(sq.dma_start(
                                            out=d_out.ap()[h][:, tt * TQ:(tt + 1) * TQ],
                                            in_=kf_sb[my % 2][:, :]))
                                    sl_kf.release(my, m_kst[my % 2])
                            sl_xnb.release(0, s_pe)
                            sl_sct.release(tt, s_dve)
                        sl_w.release(0, s_pe)

            if upto < 3:
                _finish(nc, SEMS)
                return nc

            with nc.Block() as blk:
                sl_psB5 = seeded(1)
                prev_bm = s_pe.n
                for hq in range(HPC * 8):
                    h = hq // 8
                    @blk.vector
                    def _(v, h=h, hq=hq, prev_bm=prev_bm):
                        if hq == 0:
                            v.wait_ge(s_dve.h, s_dve.n)
                        else:
                            v.wait_ge(s_pe.h, prev_bm)
                        s_dve.inc(v.tensor_mul(t0a[:, :], bq_sb[0:1, hq:hq + 1], bk_sb[0:1, h:h + 1]))
                    m_n = s_dve.n

                    @blk.scalar
                    def _(sc, m_n=m_n):
                        sc.wait_ge(s_dve.h, m_n)
                        s_act.inc(sc.activation(t0b[:, :], t0a[:, :], ACTF.Sqrt))
                    sq_n = s_act.n

                    @blk.vector
                    def _(v, sq_n=sq_n):
                        v.wait_ge(s_act.h, sq_n)
                        s_dve.inc(v.tensor_scalar(t0a[:, :], t0b[:, :], -SCL * BND_PAD, 40.0,
                                                  ALU.mult, ALU.add))
                        v.wait_ge(s_dve.h, s_dve.n)
                    bb_n = s_dve.n

                    @blk.tensor
                    def _(te, bb_n=bb_n):
                        te.wait_ge(s_dve.h, bb_n)
                        sl_psB5.acquire(te, 0)
                        s_pe.inc(te.matmul(ps[5][:, 0:1], ones_rf[:, :], t0a[:, :],
                                           start=True, stop=True))
                    bm_n = s_pe.n
                    prev_bm = bm_n

                    @blk.vector
                    def _(v, hq=hq, bm_n=bm_n):
                        v.wait_ge(s_pe.h, bm_n)
                        s_dve.inc(v.tensor_copy(bias_sb[:, hq:hq + 1], ps[5][:, 0:1]))
                    sl_psB5.release(0, s_dve)
                if dbg:
                    @blk.sync
                    def _(sq):
                        sq.wait_ge(s_dve.h, s_dve.n)
                        m_dbg.dma(sq.dma_start(out=d_bdb[:, :], in_=bias_sb[:, :]))

            with ExitStack() as st3:
                woT_sb = st3.enter_context(nc.sbuf_tensor("woT_sb", [128, HPC * C], BF16))
                msk_sb = st3.enter_context(nc.sbuf_tensor("msk_sb", [128, 1024], F32))
                kbuf = [st3.enter_context(nc.sbuf_tensor(f"kbuf{i}", [128, T], BF16)) for i in range(2)]
                q_sb = [st3.enter_context(nc.sbuf_tensor(f"q_sb{i}", [128, TQ], BF16)) for i in range(2)]
                pt_sb = [st3.enter_context(nc.sbuf_tensor(f"pt_sb{i}", [128, TQ], BF16)) for i in range(3)]
                attn_sb = st3.enter_context(nc.sbuf_tensor("attn_sb", [128, HPC * TQ], BF16))
                rb_sb = st3.enter_context(nc.sbuf_tensor("rb_sb", [1, TQ], BF16))
                rg_sb = st3.enter_context(nc.sbuf_tensor("rg_sb", [1, TQ], F32))
                rbc_sb = st3.enter_context(nc.sbuf_tensor("rbc_sb", [128, TQ], F32))
                yp_sb = [st3.enter_context(nc.sbuf_tensor(f"yp_sb{i}", [128, TQ], BF16)) for i in range(2)]
                sl_q, sl_k = Slot(2), Slot(2)
                PSAB = [0, 1, 7]
                sl_psAB = seeded(3)
                sl_pt = Slot(3)
                sl_psO = seeded(2)
                sl_psr = seeded(2)
                sl_psB = seeded(1)
                sl_attn = Slot(1)
                sl_yp = Slot(2)

                iters = [(qb, h) for qb in range(NT) for h in range(HPC)]
                ldq_n, ldk_n = {}, {}

                def p3_loads(blk, li):
                    qb, h = iters[li]

                    @blk.sync
                    def _(sq, li=li, qb=qb, h=h):
                        sl_q.acquire(sq, li)
                        m_q[li % 2].dma(sq.dma_start(out=q_sb[li % 2][:, :],
                                                     in_=d_qs.ap()[h][:, qb * TQ:(qb + 1) * TQ]))
                    ldq_n[li] = m_q[li % 2].n

                    @blk.gpsimd
                    def _(g, li=li, qb=qb, h=h):
                        sl_k.acquire(g, li)
                        pl = TQ * (qb + 1)
                        m_k[li % 2].dma(g.dma_start(out=kbuf[li % 2][:, 0:pl],
                                                    in_=d_ks.ap()[h][:, 0:pl]))
                    ldk_n[li] = m_k[li % 2].n

                with nc.Block() as blk:
                    fence(blk)

                    @blk.gpsimd
                    def _(g):
                        m_wo.dma(g.dma_start(out=woT_sb[:, :].rearrange("p (a d) -> p a d", a=HPC),
                                             in_=d_wo.ap().rearrange("(a p) d -> p a d", p=128)))
                    wo_n = m_wo.n

                    @blk.sync
                    def _(sq):
                        m_msk.dma(sq.dma_start(out=msk_sb[:, :], in_=d_msk[:, :]))
                    mk_n = m_msk.n

                    p3_loads(blk, 0)

                    kt_i = 0
                    for li, (qb, h) in enumerate(iters):
                        nkt = 4 * (qb + 1)
                        base_kt = kt_i
                        kt_i += nkt
                        smm_ns = {}

                        def emit_smm(kt, li=li, qb=qb, base_kt=base_kt, smm_ns=smm_ns):
                            my_kt = base_kt + kt

                            @blk.tensor
                            def _(te, li=li, kt=kt, my_kt=my_kt):
                                if kt == 0:
                                    te.wait_ge(m_q[li % 2].h, ldq_n[li])
                                    te.wait_ge(m_k[li % 2].h, ldk_n[li])
                                sl_psAB.acquire(te, my_kt)
                                s_pe.inc(te.matmul(ps[PSAB[my_kt % 3]][:, :],
                                                   kbuf[li % 2][:, kt * 128:(kt + 1) * 128],
                                                   q_sb[li % 2][:, :], start=True, stop=True))
                            smm_ns[kt] = s_pe.n

                        emit_smm(0)
                        if nkt > 1:
                            emit_smm(1)
                        for kt in range(nkt):
                            my_kt = base_kt + kt
                            smm_n = smm_ns[kt]

                            dve_mask_n = None
                            if kt >= 4 * qb:
                                o = kt * 128 - qb * TQ

                                @blk.vector
                                def _(v, my_kt=my_kt, o=o, smm_n=smm_n, mk_n=mk_n):
                                    v.wait_ge(s_pe.h, smm_n)
                                    if my_kt < 8:
                                        v.wait_ge(m_msk.h, mk_n)
                                    s_dve.inc(v.tensor_add(ps[PSAB[my_kt % 3]][:, :],
                                                           ps[PSAB[my_kt % 3]][:, :],
                                                           msk_sb[:, 512 - o: 1024 - o]))
                                dve_mask_n = s_dve.n

                            @blk.scalar
                            def _(sc, my_kt=my_kt, h=h, qb=qb, smm_n=smm_n, dve_mask_n=dve_mask_n):
                                sc.wait_ge(s_pe.h, smm_n)
                                if dve_mask_n is not None:
                                    sc.wait_ge(s_dve.h, dve_mask_n)
                                sl_pt.acquire(sc, my_kt)
                                with nc.allow_low_precision(reason="softmax probs bf16"):
                                    s_act.inc(sc.activation(pt_sb[my_kt % 3][:, :], ps[PSAB[my_kt % 3]][:, :],
                                                            ACTF.Exp, bias=bias_sb[:, h * 8 + qb: h * 8 + qb + 1],
                                                            scale=SCL))
                            exp_n = s_act.n
                            sl_psAB.release(my_kt, s_act)

                            if kt + 2 < nkt:
                                emit_smm(kt + 2)

                            @blk.tensor
                            def _(te, li=li, kt=kt, my_kt=my_kt, h=h, nkt=nkt, exp_n=exp_n):
                                te.wait_ge(s_act.h, exp_n)
                                if kt == 0:
                                    sl_psO.acquire(te, li)
                                    sl_psr.acquire(te, li)
                                te.matmul(ps[2 + li % 2][:, :],
                                          V_sb[:, kt * 1024 + h * 128: kt * 1024 + (h + 1) * 128],
                                          pt_sb[my_kt % 3][:, :],
                                          start=(kt == 0), stop=(kt == nkt - 1))
                                s_pe.inc(te.matmul(ps[4 + li % 2][0:1, :], ones_col[:, :],
                                                   pt_sb[my_kt % 3][:, :],
                                                   start=(kt == 0), stop=(kt == nkt - 1)))
                            sl_pt.release(my_kt, s_pe)
                        omm_n = s_pe.n
                        sl_q.release(li, s_pe)
                        sl_k.release(li, s_pe)

                        if li + 1 < len(iters):
                            p3_loads(blk, li + 1)

                        @blk.vector
                        def _(v, li=li, omm_n=omm_n):
                            v.wait_ge(s_pe.h, omm_n)
                            s_dve.inc(v.tensor_scalar(rg_sb[:, :], ps[4 + li % 2][0:1, :], 1e-38, None,
                                                      ALU.add))
                            v.wait_ge(s_dve.h, s_dve.n)
                            with nc.allow_low_precision(reason="softmax normalizer bf16"):
                                s_dve.inc(v.reciprocal(rb_sb[:, :], rg_sb[:, :]))
                            v.wait_ge(s_dve.h, s_dve.n)
                        rc_n = s_dve.n

                        @blk.tensor
                        def _(te, rc_n=rc_n):
                            te.wait_ge(s_dve.h, rc_n)
                            sl_psB.acquire(te, 0)
                            s_pe.inc(te.matmul(ps[6][:, :], ones_rb[:, :], rb_sb[:, :],
                                               start=True, stop=True))
                        bc_n = s_pe.n

                        @blk.vector
                        def _(v, li=li, h=h, bc_n=bc_n):
                            v.wait_ge(s_pe.h, bc_n)
                            v.tensor_copy(rbc_sb[:, :], ps[6][:, :])
                            if h == 0:
                                sl_attn.acquire(v, 0)
                            with nc.allow_low_precision(reason="attention out bf16"):
                                s_dve.inc(v.tensor_mul(attn_sb[:, h * TQ:(h + 1) * TQ],
                                                       ps[2 + li % 2][:, :], rbc_sb[:, :]))
                        sl_psB.release(0, s_dve)
                        sl_psO.release(li, s_dve)
                        sl_psr.release(li, s_dve)

                        if h == HPC - 1:
                            attn_n = s_dve.n
                            for dtile in range(NCC):
                                wi = qb * NCC + dtile

                                @blk.tensor
                                def _(te, qb=qb, dtile=dtile, wi=wi, attn_n=attn_n, wo_n=wo_n):
                                    if dtile == 0:
                                        te.wait_ge(s_dve.h, attn_n)
                                        if qb == 0:
                                            te.wait_ge(m_wo.h, wo_n)
                                    sl_psAB.acquire(te, wi)
                                    mm = None
                                    for cc in range(HPC):
                                        mm = te.matmul(ps[PSAB[wi % 3]][:, :],
                                                       woT_sb[:, cc * C + dtile * 128: cc * C + (dtile + 1) * 128],
                                                       attn_sb[:, cc * TQ:(cc + 1) * TQ],
                                                       start=(cc == 0), stop=(cc == HPC - 1))
                                    s_pe.inc(mm)
                                wmm_n = s_pe.n

                                @blk.vector
                                def _(v, wi=wi, wmm_n=wmm_n):
                                    v.wait_ge(s_pe.h, wmm_n)
                                    sl_yp.acquire(v, wi)
                                    with nc.allow_low_precision(reason="wo partials bf16 (RS-summed)"):
                                        s_dve.inc(v.tensor_copy(yp_sb[wi % 2][:, :], ps[PSAB[wi % 3]][:, :]))
                                wcp_n = s_dve.n
                                sl_psAB.release(wi, s_dve)

                                @blk.sync
                                def _(sq, qb=qb, dtile=dtile, wi=wi, wcp_n=wcp_n):
                                    sq.wait_ge(s_dve.h, wcp_n)
                                    m_cci[wi % 2].dma(sq.dma_start(
                                        out=d_cci.ap()[qb // 2, dtile * 128:(dtile + 1) * 128,
                                                       (qb % 2) * TQ:((qb % 2) + 1) * TQ],
                                        in_=yp_sb[wi % 2][:, :]))
                                sl_yp.release(wi, m_cci[wi % 2])
                            sl_attn.release(0, s_pe)
                    cci0_n, cci1_n = m_cci[0].n, m_cci[1].n

                    if dbg:
                        pass

        if upto < 5:
            _finish(nc, SEMS)
            return nc

        with ExitStack() as st5:
            xno = st5.enter_context(nc.sbuf_tensor("xno", [128, NCC * TOK], BF16))
            wg_sb = [st5.enter_context(nc.sbuf_tensor(f"wg_sb{i}", [128, NCC * 256], BF16)) for i in range(2)]
            wu_sb = [st5.enter_context(nc.sbuf_tensor(f"wu_sb{i}", [128, NCC * 256], BF16)) for i in range(2)]
            g_sb = [st5.enter_context(nc.sbuf_tensor(f"g_sb{i}", [128, TQ], BF16)) for i in range(2)]
            u_sb = [st5.enter_context(nc.sbuf_tensor(f"u_sb{i}", [128, TQ], BF16)) for i in range(2)]
            mid_sb = [st5.enter_context(nc.sbuf_tensor(f"mid_sb{i}", [128, TQ], BF16)) for i in range(4)]
            sl_wgu = Slot(2)
            sl_psG = seeded(2)
            sl_psU = seeded(2)
            sl_g, sl_u, sl_mid = Slot(2), Slot(2), Slot(4)
            wgu_n = {}

            def load_wgu(blk, hc):
                @blk.gpsimd
                def _(g, hc=hc):
                    sl_wgu.acquire(g, hc)
                    m_wgu[hc % 2].dma(g.dma_start(
                        out=wg_sb[hc % 2][:, :].rearrange("p (a h) -> p a h", a=NCC),
                        in_=d_wg.ap()[hc].transpose([1, 0, 2])))
                    m_wgu[hc % 2].dma(g.dma_start(
                        out=wu_sb[hc % 2][:, :].rearrange("p (a h) -> p a h", a=NCC),
                        in_=d_wu.ap()[hc].transpose([1, 0, 2])))
                wgu_n[hc] = m_wgu[hc % 2].n

            with nc.Block() as blk:
                fence(blk)

                @blk.gpsimd
                def _(g):
                    blksz = NCC * 128 * 128
                    with g.register("rk") as rk, g.register("rk2") as rk2:
                        g.reg_load(rk, idx_sb[:1, :1])
                        g.reg_mul(rk, rk, 8 * blksz)
                        for k in range(8):
                            g.reg_mov(rk2, k * blksz)
                            g.reg_add(rk2, rk2, rk)
                            m_xno.dma(g.dma_start(
                                out=xno[:, k * (NCC * 128):(k + 1) * (NCC * 128)],
                                in_=bass.AP(d_xn, rk2, [[NCC * 128, 128], [1, NCC * 128]])))
                xn_n = m_xno.n

                load_wgu(blk, 0)
                for hc in range(NHC2):
                    for hs in range(2):
                        for th in range(2):
                            i = hc * 4 + hs * 2 + th

                            @blk.tensor
                            def _(te, hc=hc, hs=hs, th=th, i=i, xn_n=xn_n):
                                te.wait_ge(m_wgu[hc % 2].h, wgu_n[hc])
                                if hc == 0:
                                    te.wait_ge(m_xno.h, xn_n)
                                sl_psG.acquire(te, i)
                                sl_psU.acquire(te, i)
                                pG = ps[2 * (i % 2)]
                                pU = ps[2 * (i % 2) + 1]
                                xnr = xno[:, :].rearrange("p (k a t) -> p k a t", k=8, a=NCC)
                                mm = None
                                for cc in range(NCC):
                                    mm = te.matmul(pG[:, :],
                                                   wg_sb[hc % 2][:, cc * 256 + hs * 128: cc * 256 + (hs + 1) * 128],
                                                   xnr[:, 4 * th:4 * th + 4, cc, :],
                                                   start=(cc == 0), stop=(cc == NCC - 1))
                                s_pe.inc(mm)
                                for cc in range(NCC):
                                    mm = te.matmul(pU[:, :],
                                                   wu_sb[hc % 2][:, cc * 256 + hs * 128: cc * 256 + (hs + 1) * 128],
                                                   xnr[:, 4 * th:4 * th + 4, cc, :],
                                                   start=(cc == 0), stop=(cc == NCC - 1))
                                s_pe.inc(mm)
                            g_mm, u_mm = s_pe.n - 1, s_pe.n

                            @blk.scalar
                            def _(sc, i=i, g_mm=g_mm):
                                sc.wait_ge(s_pe.h, g_mm)
                                sl_g.acquire(sc, i)
                                with nc.allow_low_precision(reason="ffn mid bf16"):
                                    s_act.inc(sc.activation(g_sb[i % 2][:, :], ps[2 * (i % 2)][:, :],
                                                            ACTF.Silu, scale=float(sg)))
                            sil_n = s_act.n
                            sl_psG.release(i, s_act)

                            @blk.vector
                            def _(v, i=i, u_mm=u_mm, sil_n=sil_n):
                                v.wait_ge(s_pe.h, u_mm)
                                sl_u.acquire(v, i)
                                with nc.allow_low_precision(reason="ffn mid bf16"):
                                    s_dve.inc(v.tensor_scalar(u_sb[i % 2][:, :], ps[2 * (i % 2) + 1][:, :],
                                                              float(su), None, ALU.mult))
                                    v.wait_ge(s_act.h, sil_n)
                                    v.wait_ge(s_dve.h, s_dve.n)
                                    sl_mid.acquire(v, i)
                                    s_dve.inc(v.tensor_mul(mid_sb[i % 4][:, :], g_sb[i % 2][:, :],
                                                           u_sb[i % 2][:, :]))
                            mid_n = s_dve.n
                            sl_psU.release(i, s_dve)
                            sl_g.release(i, s_dve)
                            sl_u.release(i, s_dve)

                            @blk.sync
                            def _(sq, hc=hc, hs=hs, th=th, i=i, mid_n=mid_n):
                                sq.wait_ge(s_dve.h, mid_n)
                                m_mid[i % 4].dma(sq.dma_start(
                                    out=d_mid.ap()[hc * 2 + hs][:, th * TQ:(th + 1) * TQ],
                                    in_=mid_sb[i % 4][:, :]))
                            sl_mid.release(i, m_mid[i % 4])
                    sl_wgu.release(hc, s_pe)
                    if hc + 1 < NHC2:
                        load_wgu(blk, hc + 1)

                @blk.gpsimd
                def _(g):
                    g.wait_ge(m_cci[0].h, cci0_n)
                    g.wait_ge(m_cci[1].h, cci1_n)
                    s_cc.inc(g.collective_compute(
                        "ReduceScatter", ALU.add,
                        replica_groups=[[0, 1, 2, 3], [4, 5, 6, 7]],
                        ins=[d_cci.ap()], outs=[d_cco.ap()]))
                if dbg:
                    @blk.gpsimd
                    def _(g):
                        g.wait_ge(s_cc.h, 1)
                        m_dbg.dma(g.dma_start(out=d_ccid.ap(), in_=d_cci.ap()))
                        m_dbg.dma(g.dma_start(out=d_ccod.ap(), in_=d_cco.ap()))
            mid_totals = [(m.h, m.n) for m in m_mid]

        if upto < 6:
            _finish(nc, SEMS)
            return nc

        with ExitStack() as st6:
            mid6 = st6.enter_context(nc.sbuf_tensor("mid6", [128, NHC2 * TOK], BF16))
            yh_sb = st6.enter_context(nc.sbuf_tensor("yh_sb", [128, NCC * TOK], BF16))
            wd_sb = [st6.enter_context(nc.sbuf_tensor(f"wd_sb{i}", [128, NHC2 * 128], BF16)) for i in range(2)]
            xt_sb = [st6.enter_context(nc.sbuf_tensor(f"xt_sb{i}", [128, TQ], F32)) for i in range(2)]
            ya_sb = [st6.enter_context(nc.sbuf_tensor(f"ya_sb{i}", [128, TQ], BF16)) for i in range(2)]
            tm_sb = [st6.enter_context(nc.sbuf_tensor(f"tm_sb{i}", [128, TQ], F32)) for i in range(2)]
            tn_sb = [st6.enter_context(nc.sbuf_tensor(f"tn_sb{i}", [128, TQ], F32)) for i in range(2)]
            y_sb = [st6.enter_context(nc.sbuf_tensor(f"y_sb{i}", [128, TQ], F32)) for i in range(2)]
            sl_mid6 = Slot(1)
            sl_wd = Slot(2)
            sl_psY = seeded(2)
            sl_xt, sl_ya, sl_y = Slot(2), Slot(2), Slot(2)
            wd_n = {}

            def load_wd(blk, half, dtile):
                @blk.gpsimd
                def _(g, half=half, dtile=dtile):
                    sl_wd.acquire(g, half * NCC + dtile)
                    m_wd[dtile % 2].dma(g.dma_start(
                        out=wd_sb[dtile % 2][:, :].rearrange("p (a d) -> p a d", a=NHC2),
                        in_=d_wd.ap()[dtile, half * NHC2:(half + 1) * NHC2, :, :].transpose([1, 0, 2])))
                wd_n[(half, dtile)] = m_wd[dtile % 2].n

            with nc.Block() as blk:
                fence(blk)
                load_wd(blk, 0, 0)
                for half in range(2):
                    @blk.sync
                    def _(sq, half=half):
                        if half == 0:
                            for hh, vv in mid_totals:
                                sq.wait_ge(hh, vv)
                        sl_mid6.acquire(sq, 0)
                        m_m6.dma(sq.dma_start(
                            out=mid6[:, :].rearrange("p (a t) -> p a t", a=NHC2),
                            in_=d_mid.ap()[half * NHC2:(half + 1) * NHC2, :, :].transpose([1, 0, 2])))
                    mid_ld = m_m6.n

                    for dtile in range(NCC):
                        for th in range(2):
                            i = (half * NCC + dtile) * 2 + th

                            @blk.tensor
                            def _(te, half=half, dtile=dtile, th=th, i=i, mid_ld=mid_ld):
                                te.wait_ge(m_wd[dtile % 2].h, wd_n[(half, dtile)])
                                if dtile == 0 and th == 0:
                                    te.wait_ge(m_m6.h, mid_ld)
                                sl_psY.acquire(te, i)
                                pY = ps[i % 2]
                                mm = None
                                for hcc in range(NHC2):
                                    mm = te.matmul(pY[:, :],
                                                   wd_sb[dtile % 2][:, hcc * 128:(hcc + 1) * 128],
                                                   mid6[:, hcc * TOK + th * TQ: hcc * TOK + (th + 1) * TQ],
                                                   start=(hcc == 0), stop=(hcc == NHC2 - 1))
                                s_pe.inc(mm)
                            y_mm = s_pe.n

                            if half == 0:
                                @blk.vector
                                def _(v, dtile=dtile, th=th, i=i, y_mm=y_mm):
                                    v.wait_ge(s_pe.h, y_mm)
                                    with nc.allow_low_precision(reason="half-sum bf16 staging"):
                                        s_dve.inc(v.tensor_copy(
                                            yh_sb[:, dtile * TOK + th * TQ: dtile * TOK + (th + 1) * TQ],
                                            ps[i % 2][:, :]))
                                sl_psY.release(i, s_dve)
                            else:
                                @blk.sync
                                def _(sq, dtile=dtile, th=th, i=i):
                                    sl_xt.acquire(sq, i)
                                    sl_ya.acquire(sq, i)
                                    if dtile == 0 and th == 0:
                                        sq.wait_ge(s_cc.h, 1)
                                    m_ep[i % 2].dma(sq.dma_start(
                                        out=xt_sb[i % 2][:, :],
                                        in_=d_xT[dtile * 128:(dtile + 1) * 128, th * TQ:(th + 1) * TQ]))
                                    m_ep[i % 2].dma(sq.dma_start(
                                        out=ya_sb[i % 2][:, :],
                                        in_=d_cco.ap()[dtile * 128:(dtile + 1) * 128,
                                                       th * TQ:(th + 1) * TQ]))
                                ep_ld = m_ep[i % 2].n

                                @blk.vector
                                def _(v, dtile=dtile, th=th, i=i, y_mm=y_mm, ep_ld=ep_ld):
                                    v.wait_ge(s_pe.h, y_mm)
                                    v.wait_ge(m_ep[i % 2].h, ep_ld)
                                    v.tensor_add(tm_sb[i % 2][:, :], ps[i % 2][:, :],
                                                 yh_sb[:, dtile * TOK + th * TQ: dtile * TOK + (th + 1) * TQ])
                                    v.scalar_tensor_tensor(tn_sb[i % 2][:, :], tm_sb[i % 2][:, :],
                                                           float(sd), ya_sb[i % 2][:, :], ALU.mult, ALU.add)
                                    sl_y.acquire(v, i)
                                    s_dve.inc(v.tensor_add(y_sb[i % 2][:, :], tn_sb[i % 2][:, :],
                                                           xt_sb[i % 2][:, :]))
                                yv_n = s_dve.n
                                sl_psY.release(i, s_dve)
                                sl_xt.release(i, s_dve)
                                sl_ya.release(i, s_dve)

                                @blk.sync
                                def _(sq, dtile=dtile, th=th, i=i, yv_n=yv_n):
                                    sq.wait_ge(s_dve.h, yv_n)
                                    m_y[i % 2].dma(sq.dma_start(
                                        out=d_y[dtile * 128:(dtile + 1) * 128, th * TQ:(th + 1) * TQ],
                                        in_=y_sb[i % 2][:, :]))
                                sl_y.release(i, m_y[i % 2])
                        sl_wd.release(half * NCC + dtile, s_pe)
                        nxt = (half, dtile + 1) if dtile + 1 < NCC else ((1, 0) if half == 0 else None)
                        if nxt is not None:
                            load_wd(blk, *nxt)
                    sl_mid6.release(0, s_pe)

        _finish(nc, SEMS)
    return nc


def _finish(nc, SEMS):
    with nc.Block() as blk:
        vals = [(s.h, s.n) for s in SEMS if s.n > 0]

        @blk.sync
        def _(sq):
            for h, v in vals:
                sq.wait_ge(h, v)

        @blk.gpsimd
        def _(g):
            for h, v in vals:
                g.wait_ge(h, v)


def _host_prep(inputs):
    x = np.asarray(inputs["L0_attn_x"], dtype=np.float32)
    wq = np.asarray(inputs["L0_attn_wq"], dtype=np.float32)
    wk = np.asarray(inputs["L0_attn_wk"], dtype=np.float32)
    wv = np.asarray(inputs["L0_attn_wv"], dtype=np.float32)
    wo = np.asarray(inputs["L0_attn_wo"], dtype=np.float32)
    wg = np.asarray(inputs["L0_ffn_w_gate"], dtype=np.int32)
    wu = np.asarray(inputs["L0_ffn_w_up"], dtype=np.int32)
    wd = np.asarray(inputs["L0_ffn_w_down"], dtype=np.int32)

    wqT = np.ascontiguousarray(wq.T)
    wkT = np.ascontiguousarray(wk.T)
    wvT = np.ascontiguousarray(wv.T)
    woT = np.ascontiguousarray(wo.T)
    wg_t = np.ascontiguousarray(wg.T.reshape(NCC, 128, NHC2, 256).transpose(2, 0, 1, 3))
    wu_t = np.ascontiguousarray(wu.T.reshape(NCC, 128, NHC2, 256).transpose(2, 0, 1, 3))
    wd_t = np.ascontiguousarray(wd.T.reshape(NHC, 128, NCC, 128).transpose(2, 0, 1, 3))

    pos = np.arange(T, dtype=np.float64)
    thetas = 1.0 / 10000.0 ** (np.arange(0, HD, 2, dtype=np.float64) / HD)
    pt = np.concatenate([np.outer(pos, thetas)] * 2, axis=1)
    sin_t = np.ascontiguousarray(np.sin(pt).T.astype(np.float32))
    cos_t = np.ascontiguousarray(np.cos(pt).T.astype(np.float32))

    rr = np.zeros((HD, HD), dtype=np.float32)
    for i in range(HD // 2):
        rr[i, i + HD // 2] = -1.0
        rr[i + HD // 2, i] = 1.0
    rt = np.ascontiguousarray(rr.T).astype(ml_dtypes.bfloat16)

    ii, jj = np.meshgrid(np.arange(128), np.arange(1024), indexing="ij")
    msk = np.ascontiguousarray(np.where(jj < ii + 512, -1e9, 0.0).astype(np.float32))

    x_b = [np.ascontiguousarray(x[b]) for b in range(B)]
    xT_b = [np.ascontiguousarray(x[b].T) for b in range(B)]

    in_maps = []
    for core in range(8):
        b, j = core // 4, core % 4
        m = {
            "x_tok": x_b[b],
            "xT_own": np.ascontiguousarray(xT_b[b][:, TOK * j: TOK * (j + 1)]),
            "wvT_g": np.ascontiguousarray(wvT[:, HPC * HD * j: HPC * HD * (j + 1)]),
            "woT_mine": np.ascontiguousarray(woT[HPC * HD * j: HPC * HD * (j + 1), :]),
            "wg_t": wg_t, "wu_t": wu_t, "wd_t": wd_t,
            "sin_t": sin_t, "cos_t": cos_t, "rope_rt": rt, "msk": msk,
            "quarter_idx": np.array([[j]], dtype=np.int32),
        }
        for p in range(2):
            d0 = HD * (HPC * j + HPP * p)
            m[f"wkT_p{p}"] = np.ascontiguousarray(wkT[:, d0:d0 + HPP * HD])
            m[f"wqT_p{p}"] = np.ascontiguousarray(wqT[:, d0:d0 + HPP * HD])
        in_maps.append(m)
    return in_maps


def kernel(**inputs):
    sg = float(np.asarray(inputs["L0_ffn_scale_gate"]))
    su = float(np.asarray(inputs["L0_ffn_scale_up"]))
    sd = float(np.asarray(inputs["L0_ffn_scale_down"]))
    in_maps = _host_prep(inputs)
    nc = build(sg, su, sd, upto=_UPTO, dbg=_DBG_OUT)
    res = run_bass_kernel_spmd(nc, in_maps, core_ids=list(range(8)))
    out = np.empty((B, T, C), dtype=np.float32)
    for core in range(8):
        b, j = core // 4, core % 4
        y = np.asarray(res.results[core]["y_out"])
        out[b, TOK * j: TOK * (j + 1), :] = y.T
    if _DBG_OUT:
        kernel._last_results = res
    return out
